# version 17
# baseline (speedup 1.0000x reference)
import math
import numpy as np
import ml_dtypes
import concourse.bass as bass
import concourse.mybir as mybir
from concourse.bass_utils import run_bass_kernel_spmd

F32 = mybir.dt.float32
BF16 = mybir.dt.bfloat16
AF = mybir.ActivationFunctionType
ALU = mybir.AluOpType
AX = mybir.AxisListType

D = 4096
TOK = 1024
KC = 32
QKVC = 48
NCORE = 8
G = 4
SEQN = 2
QJN = 8
MASKW = 4480
SCALE = 1.0 / math.sqrt(128.0)
EPS = 1e-5
BIG = 1e30

ENGS = ("sync", "scalar", "vector", "tensor", "gpsimd")


class Sem:
    __slots__ = ("h", "n")

    def __init__(self, nc, name):
        self.h = nc.alloc_semaphore(name)
        self.n = 0


class Prog:
    def __init__(self, nc):
        self.nc = nc
        self.q = {e: [] for e in ENGS}

    def on(self, eng, fn):
        self.q[eng].append(fn)

    def wait(self, eng, sem, val=None):
        v = sem.n if val is None else val
        assert 0 <= v <= sem.n, f"wait on future value {v} > {sem.n}"
        if v <= 0:
            return
        self.q[eng].append(lambda e, h=sem.h, v=v: e.wait_ge(h, v))

    def dma(self, eng, out, in_, sem):
        sem.n += 16
        self.q[eng].append(
            lambda e, o=out, i=in_, h=sem.h: e.dma_start(out=o, in_=i).then_inc(h, 16))

    def dma_pid(self, eng, dst, make_src, sem):
        sem.n += 16

        def f(e, d=dst, ms=make_src, h=sem.h):
            if not hasattr(e, "_pid_cache"):
                e._pid_cache = e.partition_id()
            e.dma_start(out=d, in_=ms(e._pid_cache)).then_inc(h, 16)
        self.q[eng].append(f)

    def mm(self, out, lhsT, rhs, start, stop, sem=None):
        if sem is not None:
            sem.n += 1
        h = sem.h if sem is not None else None

        def f(e, o=out, l=lhsT, r=rhs, s=start, t=stop, h=h):
            ins = e.matmul(o, l, r, start=s, stop=t)
            if h is not None:
                ins.then_inc(h, 1)
        self.q["tensor"].append(f)

    def trans(self, out, in_, ident, sem=None):
        if sem is not None:
            sem.n += 1
        h = sem.h if sem is not None else None

        def f(e, o=out, i=in_, d=ident, h=h):
            ins = e.transpose(o, i, d)
            if h is not None:
                ins.then_inc(h, 1)
        self.q["tensor"].append(f)

    def act(self, out, in_, func, bias=0.0, scale=1.0, sem=None):
        if sem is not None:
            sem.n += 1
        h = sem.h if sem is not None else None

        def f(e, o=out, i=in_, fu=func, b=bias, s=scale, h=h):
            ins = e.activation(out=o, in_=i, func=fu, bias=b, scale=s)
            if h is not None:
                ins.then_inc(h, 1)
        self.q["scalar"].append(f)

    def smul(self, out, in_, c, sem=None):
        if sem is not None:
            sem.n += 1
        h = sem.h if sem is not None else None

        def f(e, o=out, i=in_, c=c, h=h):
            ins = e.mul(o, i, c)
            if h is not None:
                ins.then_inc(h, 1)
        self.q["scalar"].append(f)

    def v(self, opname, *args, sem=None, **kw):
        if sem is not None:
            sem.n += 1
        h = sem.h if sem is not None else None

        def f(e, opname=opname, args=args, kw=kw, h=h):
            ins = getattr(e, opname)(*args, **kw)
            if h is not None:
                ins.then_inc(h, 1)
        self.q["vector"].append(f)

    def collective(self, kind, replica_groups, ins, outs, sem):
        sem.n += 1

        def f(e, k=kind, rg=replica_groups, i=ins, o=outs, h=sem.h):
            e.collective_compute(k, ALU.bypass, replica_groups=rg, ins=i, outs=o).then_inc(h, 1)
        self.q["gpsimd"].append(f)

    def run(self):
        with self.nc.Block() as block:
            for eng in ENGS:
                fns = self.q[eng]
                getattr(block, eng)(lambda e, fns=fns: [fn(e) for fn in fns])


def _alibi_slopes(n):
    cp2 = 2 ** math.floor(math.log2(n))
    base = 2.0 ** (-8.0 / cp2)
    slopes = base ** np.arange(1, cp2 + 1, dtype=np.float32)
    if cp2 != n:
        extra_base = 2.0 ** (-4.0 / cp2)
        extra = extra_base ** np.arange(1, 1 + 2 * (n - cp2), 2, dtype=np.float32)
        slopes = np.concatenate([slopes, extra])
    return slopes.astype(np.float32)


def build_program():
    nc = bass.Bass("TRN2", target_bir_lowering=False)

    xT = nc.dram_tensor("xT", [KC, 128, TOK], F32, kind="ExternalInput").ap()
    wqkv_t = nc.dram_tensor("wqkv_t", [QKVC, 128, KC, 128], F32, kind="ExternalInput").ap()
    wo_t = nc.dram_tensor("wo_t", [KC, 128, KC, 128], F32, kind="ExternalInput").ap()
    wbase_t = nc.dram_tensor("wbase_t", [KC, 128, KC, 128], F32, kind="ExternalInput").ap()
    a_t_in = nc.dram_tensor("a_t", [KC, 128, 128], F32, kind="ExternalInput").ap()
    b_t_in = nc.dram_tensor("b_t", [KC, 128, 128], F32, kind="ExternalInput").ap()
    wg_in = nc.dram_tensor("wg_t", [KC, 128, 8], F32, kind="ExternalInput").ap()
    bb_in = nc.dram_tensor("bb", [128, KC], F32, kind="ExternalInput").ap()
    masks_in = nc.dram_tensor("masks", [G, 128, MASKW], BF16, kind="ExternalInput").ap()
    ones_in = nc.dram_tensor("ones_c", [128, 128], F32, kind="ExternalInput").ap()
    ident_in = nc.dram_tensor("ident_c", [128, 128], F32, kind="ExternalInput").ap()
    emat_in = nc.dram_tensor("emat_c", [8, 128], F32, kind="ExternalInput").ap()

    y_out = nc.dram_tensor("y", [KC, 128, TOK], F32, kind="ExternalOutput").ap()

    r1d = nc.dram_tensor("r1d", [KC, 128, TOK], F32).ap()
    r2d = nc.dram_tensor("r2d", [KC, 128, TOK], F32).ap()
    ag1_in = nc.dram_tensor("ag1_in", [QKVC * 128, TOK], BF16)
    ag1_out = nc.dram_tensor("ag1_out", [NCORE * QKVC * 128, TOK], BF16, addr_space="Shared")
    ag2_in = nc.dram_tensor("ag2_in", [512, NCORE * TOK], BF16)
    ag2_out = nc.dram_tensor("ag2_out", [NCORE * 512, NCORE * TOK], BF16, addr_space="Shared")
    qstage = nc.dram_tensor("qstage", [G, NCORE, 128, TOK], BF16).ap()
    astage = nc.dram_tensor("astage", [NCORE, 512, TOK], BF16).ap()

    sb = nc.alloc_sbuf_tensor
    xn = sb("xn", [128, KC * TOK], BF16)[:]
    wst = sb("wst", [128, 2 * 4096], BF16)[:]
    ksb = sb("ksb", [128, 2 * 4096], BF16)[:]
    vt = sb("vt", [128, 2 * 4096], BF16)[:]
    maskb = sb("maskb", [128, MASKW], BF16)[:]
    a_t = sb("a_tb", [128, KC * 128], BF16)[:]
    b_t = sb("b_tb", [128, KC * 128], BF16)[:]
    wg_t = sb("wg_tb", [128, KC * 8], BF16)[:]
    hsb = sb("hsb", [128, TOK], F32)[:]
    hw = sb("hw", [128, TOK], BF16)[:]
    ltok = sb("ltok", [128, 64], F32)[:]
    m1b = sb("m1b", [128, 8], F32)[:]
    m2b = sb("m2b", [128, 8], F32)[:]
    tm1 = sb("tm1", [128, 64], F32)[:]
    tm2 = sb("tm2", [128, 64], F32)[:]
    l2b = sb("l2b", [128, 64], F32)[:]
    db_ = l2b
    emb = sb("emb", [128, 64], F32)[:]
    edb = tm2
    zb8 = sb("zb8", [128, 8], F32)[:]
    zr8 = sb("zr8", [128, 8], F32)[:]
    rwb = sb("rwb", [128, 64], BF16)[:]
    rwT = sb("rwT", [8, TOK], BF16)[:]
    mstat = sb("mstat", [128, TOK], F32)[:]
    rstd = sb("rstd", [128, TOK], F32)[:]
    rin = sb("rin", [128, 2 * TOK], F32)[:]
    rosb = sb("rosb", [128, 2 * TOK], F32)[:]
    rb16 = sb("rb16", [128, 2 * TOK], BF16)[:]
    sqb = sb("sqb", [128, 2 * TOK], BF16)[:]
    qsl = sb("qsl", [128, 2 * 512], BF16)[:]
    q2b = sb("q2b", [128, 2 * 512], BF16)[:]
    esb = sb("esb", [128, 2 * 512], F32)[:]
    pbf = sb("pbf", [128, 2 * 1536], BF16)[:]
    ebf = sb("ebf", [128, 2 * 1536], BF16)[:]
    zrb = sb("zrb", [128, 2 * 512], F32)[:]
    zlnb = sb("zlnb", [128, 512], F32)[:]
    aob = sb("aob", [128, 2 * 512], BF16)[:]
    negb = mstat[:, 0:64]
    bq1 = sb("bq1", [128, 4], F32)[:]
    k2maxs = sb("k2maxs", [128, 2], F32)[:]
    k2cols = sb("k2cols", [128, 16], F32)[:]
    k2max = sb("k2max", [128, 2], F32)[:]
    ones = sb("ones", [128, 128], BF16)[:]
    identb = sb("identb", [128, 128], BF16)[:]
    identf = sb("identf", [128, 8], F32)[:]
    emat = sb("emat", [8, 128], BF16)[:]
    halfb = sb("halfb", [128, KC], F32)[:]
    scr8k = wst[:, 0:4096]
    gsb = esb[0:8, :]

    ps = nc.alloc_psum_tensor("ps", [128, 8 * 512], F32)[:]

    def bank(i, w=1):
        return ps[:, i * 512:(i + w) * 512]

    p = Prog(nc)
    sMM = Sem(nc, "sMM")
    sT2 = Sem(nc, "sT2")
    sEV = Sem(nc, "sEV")
    sSE = Sem(nc, "sSE")
    sCC = Sem(nc, "sCC")
    sW = [Sem(nc, "sW0"), Sem(nc, "sW1")]
    sR = [Sem(nc, "sR0"), Sem(nc, "sR1")]
    sXB = [Sem(nc, "sXB0"), Sem(nc, "sXB1")]
    sQ = [Sem(nc, "sQ0"), Sem(nc, "sQ1")]
    sSR = [Sem(nc, "sSR0"), Sem(nc, "sSR1")]
    sSQ = [Sem(nc, "sSQ0"), Sem(nc, "sSQ1")]
    sSA = [Sem(nc, "sSA0"), Sem(nc, "sSA1")]
    sC = Sem(nc, "sC")
    sCG = Sem(nc, "sCG")
    sK = Sem(nc, "sK")
    sV = Sem(nc, "sV")
    sQS = Sem(nc, "sQS")
    sMK = Sem(nc, "sMK")
    sAT = Sem(nc, "sAT")
    sAS = Sem(nc, "sAS")

    p.dma("gpsimd", ones[:, :], ones_in, sCG)
    p.dma("gpsimd", identb[:, :], ident_in, sCG)
    p.dma("gpsimd", emat[:, :], emat_in, sCG)
    p.dma("gpsimd", a_t[:].rearrange("p (k f) -> p k f", k=KC), a_t_in.transpose([1, 0, 2]), sCG)
    p.dma("gpsimd", b_t[:].rearrange("p (k f) -> p k f", k=KC), b_t_in.transpose([1, 0, 2]), sCG)
    p.dma("gpsimd", wg_t[:].rearrange("p (k f) -> p k f", k=KC), wg_in.transpose([1, 0, 2]), sCG)
    p.dma("sync", identf[:, :], ident_in[:, 0:8], sC)
    p.dma("sync", esb[:, 0:KC], bb_in, sC)
    p.wait("vector", sC)
    p.v("tensor_scalar_mul", halfb[:, :], esb[:, 0:KC], 0.5, sem=sEV)
    HALFB_EV = sEV.n

    statmm_done = []
    for k in range(KC):
        buf = k % 2
        if k >= 2:
            p.wait("gpsimd", sMM, statmm_done[k - 2])
        p.dma("gpsimd", rb16[:, buf * TOK:(buf + 1) * TOK], xT[k], sXB[buf])
        lw_here = sXB[buf].n
        p.wait("scalar", sXB[buf], lw_here)
        if k >= 2:
            p.wait("scalar", sMM, statmm_done[k - 2])
        p.act(sqb[:, buf * TOK:(buf + 1) * TOK], rb16[:, buf * TOK:(buf + 1) * TOK], AF.Square, sem=sSE)
        se_here = sSE.n
        if k == 0:
            p.wait("tensor", sCG)
        p.wait("tensor", sXB[buf], lw_here)
        p.wait("tensor", sSE, se_here)
        for hf in range(2):
            rb = rb16[:, buf * TOK + hf * 512: buf * TOK + (hf + 1) * 512]
            sq = sqb[:, buf * TOK + hf * 512: buf * TOK + (hf + 1) * 512]
            p.mm(bank(0 + hf), ones[:, :], rb, start=(k == 0), stop=(k == KC - 1))
            p.mm(bank(2 + hf), ones[:, :], sq, start=(k == 0), stop=(k == KC - 1),
                 sem=(sMM if hf == 1 else None))
        statmm_done.append(sMM.n)

    state = {"rstd_se": 0}

    def ln_finalize(store_sems=None):
        T1 = esb[:, 0:512 * 2]
        T2 = rosb[:, 0:TOK]
        p.wait("vector", sMM)
        if store_sems is not None:
            for s_ in store_sems:
                p.wait("vector", s_)
        p.v("tensor_scalar_mul", mstat[:, :], bank(0, 2), 1.0 / D, sem=sEV)
        p.wait("vector", sEV)
        p.v("tensor_mul", T1, mstat[:, :], mstat[:, :], sem=sEV)
        p.wait("vector", sEV)
        p.v("scalar_tensor_tensor", T2, bank(2, 2), 1.0 / D, T1,
            op0=ALU.mult, op1=ALU.subtract, sem=sEV)
        p.wait("vector", sEV)
        p.v("tensor_scalar_add", T1, T2, EPS, sem=sEV)
        p.wait("vector", sEV)
        p.v("reciprocal", T2, T1, sem=sEV)
        p.wait("scalar", sEV)
        p.act(rstd[:, :], T2, AF.Sqrt, sem=sSE)
        state["rstd_se"] = sSE.n

    ln_finalize()

    def ln_normalize(src_dram, store_gates=None, xn_done_mm=None, rosb_gate=None):
        ev_done = []

        def T(k):
            return esb[:, 0:TOK] if k % 2 == 0 else rosb[:, 0:TOK]

        def emit_mult(k):
            p.v("tensor_mul", xn[:, k * TOK:(k + 1) * TOK], T(k), rstd[:, :], sem=sEV)
            ev_done.append(sEV.n)

        for k in range(KC):
            buf = k % 2
            lq = "sync" if buf == 0 else "scalar"
            if k >= 2:
                p.wait(lq, sEV, ev_done[k - 2])
            else:
                p.wait(lq, sEV)
            if store_gates is not None:
                p.wait(lq, sSR[k % 2], store_gates[k])
            p.dma(lq, rin[:, buf * TOK:(buf + 1) * TOK], src_dram[k], sR[buf])
            ls_here = sR[buf].n
            p.wait("vector", sR[buf], ls_here)
            if k == 0:
                p.wait("vector", sSE, state["rstd_se"])
                if xn_done_mm is not None:
                    p.wait("vector", sMM, xn_done_mm)
            if k == 1 and rosb_gate is not None:
                for s_, v_ in rosb_gate:
                    p.wait("vector", s_, v_)
            p.v("tensor_sub", T(k), rin[:, buf * TOK:(buf + 1) * TOK], mstat[:, :])
            if k >= 1:
                emit_mult(k - 1)
        emit_mult(KC - 1)
        return ev_done

    xn_ev = ln_normalize(xT)

    def ffn_block(xn_ev, r_src_dram, r_gates, out_dram, do_stats, w_gate0=0):
        gate_ev = 0
        for hf in range(2):
            for k in range(KC):
                if hf == 0:
                    p.wait("tensor", sEV, xn_ev[k])
                if k == 0 and hf == 1:
                    p.wait("tensor", sEV, gate_ev)
                xk = xn[:, k * TOK + hf * 512: k * TOK + (hf + 1) * 512]
                p.mm(ps[0:8, 4 * 512:4 * 512 + 512], wg_t[:, k * 8:(k + 1) * 8], xk,
                     start=(k == 0), stop=(k == KC - 1))
                p.mm(bank(5), a_t[:, k * 128:(k + 1) * 128], xk,
                     start=(k == 0), stop=(k == KC - 1), sem=(sMM if k == KC - 1 else None))
            mm_here = sMM.n
            p.wait("vector", sMM, mm_here)
            p.v("tensor_copy", gsb[0:8, hf * 512:(hf + 1) * 512], ps[0:8, 4 * 512:4 * 512 + 512], sem=sEV)
            p.wait("vector", sEV)
            p.v("tensor_copy", hsb[:, hf * 512:(hf + 1) * 512], bank(5), sem=sEV)
            p.wait("vector", sEV)
            gate_ev = sEV.n

        p.wait("tensor", sEV, gate_ev)
        for t in range(8):
            p.trans(ps[:, 6 * 512 + t * 8: 6 * 512 + (t + 1) * 8], gsb[0:8, t * 128:(t + 1) * 128],
                    identf[0:8, 0:8], sem=(sMM if t == 7 else None))
        p.wait("vector", sMM)
        p.v("tensor_copy", ltok[:, :], ps[:, 6 * 512:6 * 512 + 64], sem=sEV)
        p.wait("vector", sEV)

        def l3(apx):
            return apx.rearrange("p (b e) -> p b e", b=8)

        def bc(ap8):
            return ap8.unsqueeze(2).broadcast_to([128, 8, 8])

        p.v("reduce_max", m1b[:, :].unsqueeze(2), l3(ltok[:, :]), axis=AX.X, sem=sEV)
        p.wait("vector", sEV)
        p.v("tensor_tensor", l3(tm1[:, :]), l3(ltok[:, :]), bc(m1b[:, :]), op=ALU.is_equal, sem=sEV)
        p.wait("vector", sEV)
        p.v("scalar_tensor_tensor", l2b[:, :], tm1[:, :], -BIG, ltok[:, :],
            op0=ALU.mult, op1=ALU.add, sem=sEV)
        p.wait("vector", sEV)
        p.v("reduce_max", m2b[:, :].unsqueeze(2), l3(l2b[:, :]), axis=AX.X, sem=sEV)
        p.wait("vector", sEV)
        p.v("tensor_tensor", l3(tm2[:, :]), l3(l2b[:, :]), bc(m2b[:, :]), op=ALU.is_equal, sem=sEV)
        p.wait("vector", sEV)
        p.v("tensor_add", tm1[:, :], tm1[:, :], tm2[:, :], sem=sEV)
        p.wait("vector", sEV)
        p.v("tensor_tensor", l3(db_[:, :]), l3(ltok[:, :]), bc(m1b[:, :]), op=ALU.subtract, sem=sEV)
        p.wait("scalar", sEV)
        p.act(edb[:, :], db_[:, :], AF.Exp, sem=sSE)
        p.wait("vector", sSE)
        p.v("tensor_mul", emb[:, :], edb[:, :], tm1[:, :], sem=sEV)
        p.wait("vector", sEV)
        p.v("reduce_sum", zb8[:, :].unsqueeze(2), l3(emb[:, :]), axis=AX.X, sem=sEV)
        p.wait("vector", sEV)
        p.v("reciprocal", zr8[:, :], zb8[:, :], sem=sEV)
        p.wait("vector", sEV)
        p.v("tensor_tensor", l3(rwb[:, :]), l3(emb[:, :]), bc(zr8[:, :]), op=ALU.mult, sem=sEV)
        rw_ev = sEV.n
        p.wait("tensor", sEV, rw_ev)
        rwT_ps = ps[0:8, 6 * 512:7 * 512].bitcast(BF16)
        for t in range(8):
            p.trans(rwT_ps[:, t * 128:(t + 1) * 128], rwb[:, t * 8:(t + 1) * 8], identb[:, :],
                    sem=(sMM if t == 7 else None))
        p.wait("vector", sMM)
        p.v("tensor_copy", rwT[0:8, :], rwT_ps, sem=sEV)
        p.wait("vector", sEV)
        p.wait("tensor", sEV)
        p.mm(bank(6), emat[0:8, :], rwT[0:8, 0:512], start=True, stop=True)
        p.mm(bank(7), emat[0:8, :], rwT[0:8, 512:1024], start=True, stop=True, sem=sMM)
        p.wait("vector", sMM)
        p.wait("vector", sEV, HALFB_EV)
        p.v("tensor_mul", hw[:, 0:512], hsb[:, 0:512], bank(6))
        p.v("tensor_mul", hw[:, 512:1024], hsb[:, 512:1024], bank(7), sem=sEV)
        hw_ev = sEV.n

        evac_ev = [0] * KC
        store_st = [0] * KC
        se_cp = [0] * KC
        stats_mm = [0] * KC
        mm_done = [0] * KC
        for m in range(KC):
            buf = m % 2
            ob = 6 if buf == 0 else 4
            if m >= 2:
                p.wait("gpsimd", sMM, mm_done[m - 2])
            elif m == 0:
                p.wait("gpsimd", sMM, w_gate0)
            p.dma("gpsimd", wst[:, buf * 4096:(buf + 1) * 4096].rearrange("p (k f) -> p k f", k=KC),
                  wbase_t[m], sW[buf])
            w_lw = sW[buf].n
            if m >= 2:
                p.wait("sync", sEV, evac_ev[m - 2])
            else:
                p.wait("sync", sEV)
            if r_gates is not None:
                p.wait("sync", sSR[buf], r_gates[m])
            p.dma("sync", rin[:, buf * TOK:(buf + 1) * TOK], r_src_dram[m], sR[buf])
            r_ls = sR[buf].n
            p.wait("tensor", sW[buf], w_lw)
            if m >= 2:
                p.wait("tensor", sEV, evac_ev[m - 2])
            if m <= 1:
                p.wait("tensor", sEV, hw_ev)
            for k in range(KC):
                for hf in range(2):
                    p.mm(bank(ob + hf), wst[:, buf * 4096 + k * 128: buf * 4096 + (k + 1) * 128],
                         xn[:, k * TOK + hf * 512: k * TOK + (hf + 1) * 512],
                         start=(k == 0), stop=False)
            for hf in range(2):
                p.mm(bank(ob + hf), b_t[:, m * 128:(m + 1) * 128], hw[:, hf * 512:(hf + 1) * 512],
                     start=False, stop=True, sem=(sMM if hf == 1 else None))
            mm_done[m] = sMM.n
            p.wait("vector", sMM, mm_done[m])
            p.wait("vector", sR[buf], r_ls)
            if m >= 2:
                p.wait("vector", sSR[buf], store_st[m - 2])
                if do_stats:
                    p.wait("vector", sSE, se_cp[m - 2])
            p.v("tensor_scalar", esb[:, 0:TOK], bank(ob, 2), 0.5, halfb[:, m:m + 1],
                op0=ALU.mult, op1=ALU.add)
            p.v("tensor_add", rosb[:, buf * TOK:(buf + 1) * TOK], esb[:, 0:TOK],
                rin[:, buf * TOK:(buf + 1) * TOK], sem=sEV)
            evac_ev[m] = sEV.n
            p.wait("scalar", sEV, evac_ev[m])
            p.dma("scalar", out_dram[m], rosb[:, buf * TOK:(buf + 1) * TOK], sSR[buf])
            store_st[m] = sSR[buf].n
            if do_stats:
                if m >= 2:
                    p.wait("scalar", sMM, stats_mm[m - 2])
                p.act(rb16[:, buf * TOK:(buf + 1) * TOK], rosb[:, buf * TOK:(buf + 1) * TOK], AF.Copy)
                p.act(sqb[:, buf * TOK:(buf + 1) * TOK], rosb[:, buf * TOK:(buf + 1) * TOK],
                      AF.Square, sem=sSE)
                se_cp[m] = sSE.n

                def emit_stats(mm_):
                    bb_ = mm_ % 2
                    p.wait("tensor", sSE, se_cp[mm_])
                    for hf in range(2):
                        rb = rb16[:, bb_ * TOK + hf * 512: bb_ * TOK + (hf + 1) * 512]
                        sq = sqb[:, bb_ * TOK + hf * 512: bb_ * TOK + (hf + 1) * 512]
                        p.mm(bank(0 + hf), ones[:, :], rb, start=(mm_ == 0), stop=(mm_ == KC - 1))
                        p.mm(bank(2 + hf), ones[:, :], sq, start=(mm_ == 0), stop=(mm_ == KC - 1),
                             sem=(sMM if hf == 1 else None))
                    stats_mm[mm_] = sMM.n
                if m >= 1:
                    emit_stats(m - 1)
                if m == KC - 1:
                    emit_stats(m)
        return evac_ev, store_st, mm_done, stats_mm

    _, b_store, b_mm, b_stats = ffn_block(xn_ev, xT, None, r1d, do_stats=True)

    ln_finalize(store_sems=[sSR[0], sSR[1]])
    xn_ev = ln_normalize(r1d, store_gates=b_store, xn_done_mm=b_mm[-1],
                         rosb_gate=[(sSR[0], b_store[KC - 2]), (sSR[1], b_store[KC - 1])])

    qkv_mm = [0] * QKVC
    qkv_ev = [0] * QKVC
    qkv_st = [0] * QKVC
    for m in range(QKVC):
        buf = m % 2
        ob = 0 if buf == 0 else 2
        if m >= 2:
            p.wait("gpsimd", sMM, qkv_mm[m - 2])
        elif m == 0:
            p.wait("gpsimd", sMM, b_mm[-1])
        p.dma("gpsimd", wst[:, buf * 4096:(buf + 1) * 4096].rearrange("p (k f) -> p k f", k=KC),
              wqkv_t[m], sW[buf])
        w_lw = sW[buf].n
        p.wait("tensor", sW[buf], w_lw)
        if m >= 2:
            p.wait("tensor", sEV, qkv_ev[m - 2])
        for k in range(KC):
            if m == 0:
                p.wait("tensor", sEV, xn_ev[k])
            for hf in range(2):
                p.mm(bank(ob + hf), wst[:, buf * 4096 + k * 128: buf * 4096 + (k + 1) * 128],
                     xn[:, k * TOK + hf * 512: k * TOK + (hf + 1) * 512],
                     start=(k == 0), stop=(k == KC - 1),
                     sem=(sMM if (k == KC - 1 and hf == 1) else None))
        qkv_mm[m] = sMM.n
        p.wait("vector", sMM, qkv_mm[m])
        if m == 0:
            p.wait("vector", sSE)
        if m >= 2:
            p.wait("vector", sSQ[buf], qkv_st[m - 2])
        p.v("tensor_copy", rb16[:, buf * TOK:(buf + 1) * TOK], bank(ob, 2), sem=sEV)
        qkv_ev[m] = sEV.n
        p.wait("scalar", sEV, qkv_ev[m])
        p.dma("scalar", ag1_in.ap()[m * 128:(m + 1) * 128, :], rb16[:, buf * TOK:(buf + 1) * TOK],
              sSQ[buf])
        qkv_st[m] = sSQ[buf].n

    p.wait("gpsimd", sSQ[0])
    p.wait("gpsimd", sSQ[1])
    p.collective("AllGather", [list(range(NCORE))], [ag1_in.ap().opt()], [ag1_out.ap().opt()], sCC)
    CC1 = sCC.n

    agv3 = ag1_out.ap().rearrange("(r x) t -> r x t", r=NCORE)
    p.wait("sync", sCC, CC1)
    p.dma_pid("sync", ksb[:].rearrange("p (r t) -> p r t", r=8),
              lambda pid: agv3[:, bass.ts(32 + pid, 128), :].transpose([1, 0, 2]), sK)
    for g in range(G):
        p.dma_pid("sync", qstage[g],
                  lambda pid, g=g: agv3[:, bass.ts(4 * pid + g, 128), :], sQS)

    vt_ev = []
    k2_ev = []
    for seq in range(SEQN):
        if seq == 1:
            p.wait("sync", sMM)
        p.dma_pid("sync", scr8k[:].rearrange("p (r t) -> p r t", r=4),
                  lambda pid, seq=seq: agv3[4 * seq:4 * seq + 4,
                                            bass.ts(40 + pid, 128), :].transpose([1, 0, 2]), sV)
        v_ls = sV.n
        vwin = ps[:, 7 * 512:8 * 512].bitcast(BF16)
        p.wait("tensor", sV, v_ls)
        for grp in range(4):
            if grp > 0 or seq > 0:
                p.wait("tensor", sEV)
            for t in range(8):
                kc = grp * 8 + t
                p.trans(vwin[:, t * 128:(t + 1) * 128], scr8k[:, kc * 128:(kc + 1) * 128],
                        identb[:, :], sem=(sMM if t == 7 else None))
            p.wait("vector", sMM)
            p.v("tensor_copy", vt[:, seq * 4096 + grp * 1024: seq * 4096 + (grp + 1) * 1024],
                vwin, sem=sEV)
            p.wait("vector", sEV)
        vt_ev.append(sEV.n)
        p.wait("scalar", sMM)
        p.wait("scalar", sK)
        p.act(scr8k[:, :], ksb[:, seq * 4096:(seq + 1) * 4096], AF.Square, sem=sSE)
        k2sq = sSE.n
        p.wait("tensor", sSE, k2sq)
        for s8 in range(8):
            if s8 > 0 or seq > 0:
                p.wait("tensor", sEV)
            p.mm(bank(6), ones[:, :], scr8k[:, s8 * 512:(s8 + 1) * 512], start=True, stop=True, sem=sMM)
            p.wait("vector", sMM)
            p.v("reduce_max", k2cols[:, seq * 8 + s8: seq * 8 + s8 + 1].unsqueeze(2),
                bank(6).unsqueeze(1), axis=AX.X, sem=sEV)
            p.wait("vector", sEV)
        p.v("reduce_max", k2max[:, seq:seq + 1].unsqueeze(2),
            k2cols[:, seq * 8:(seq + 1) * 8].unsqueeze(1), axis=AX.X, sem=sEV)
        p.wait("vector", sEV)
        k2_ev.append(sEV.n)

    NUNIT = G * SEQN * QJN

    def unit_coords(uu):
        g_, r_ = divmod(uu, SEQN * QJN)
        seq_, qj_ = divmod(r_, QJN)
        return g_, seq_, qj_

    p.wait("vector", sEV, k2_ev[1])
    p.v("tensor_scalar_mul", k2maxs[:, :], k2max[:, :], -SCALE / 2.0, sem=sEV)
    p.wait("vector", sEV)
    p.wait("sync", sQS, 16 * G)
    bq_loads = [0] * NUNIT
    q2_se = [0] * NUNIT
    bmm = [0] * NUNIT
    bq_ev = [0] * NUNIT
    for b in range(NUNIT):
        g_, seq_, qj_ = unit_coords(b)
        buf_ = b % 2
        if b >= 2:
            p.wait("sync", sSE, q2_se[b - 2])
        p.dma("sync", qsl[:, buf_ * 512:(buf_ + 1) * 512],
              qstage[g_, (4 * seq_ + qj_ // 2), :, (qj_ % 2) * 512:(qj_ % 2 + 1) * 512], sQ[buf_])
        bq_loads[b] = sQ[buf_].n
        p.wait("scalar", sQ[buf_], bq_loads[b])
        if b >= 2:
            p.wait("scalar", sMM, bmm[b - 2])
        p.act(q2b[:, buf_ * 512:(buf_ + 1) * 512], qsl[:, buf_ * 512:(buf_ + 1) * 512],
              AF.Square, sem=sSE)
        q2_se[b] = sSE.n
        p.wait("tensor", sSE, q2_se[b])
        if b < 2:
            p.wait("tensor", sEV, qkv_ev[-1])
        else:
            p.wait("tensor", sEV, bq_ev[b - 2])
        p.mm(bank(b % 2), ones[:, :], q2b[:, buf_ * 512:(buf_ + 1) * 512],
             start=True, stop=True, sem=sMM)
        bmm[b] = sMM.n
        p.wait("vector", sMM, bmm[b])
        p.v("reduce_max", bq1[:, (b % 4):(b % 4) + 1].unsqueeze(2), bank(b % 2).unsqueeze(1),
            axis=AX.X, sem=sEV)
        p.wait("vector", sEV)
        p.v("scalar_tensor_tensor", negb[:, b:b + 1], bq1[:, (b % 4):(b % 4) + 1], -SCALE / 2.0,
            k2maxs[:, seq_:seq_ + 1], op0=ALU.mult, op1=ALU.add, sem=sEV)
        bq_ev[b] = sEV.n
        p.wait("vector", sEV)
    bounds_done_ev = sEV.n

    tiles_done_mm = [0] * NUNIT
    unit_ev = [0] * NUNIT
    unit_st = [0] * NUNIT
    ln_se = [0] * NUNIT
    exp_hist = []
    mult_hist = []
    avz_hist = []

    def emit_qload(uu):
        g_, seq_, qj_ = unit_coords(uu)
        buf_ = uu % 2
        if uu >= 2:
            p.wait("sync", sMM, tiles_done_mm[uu - 2])
        else:
            p.wait("sync", sSE, q2_se[NUNIT - 2 + uu])
        p.dma("sync", qsl[:, buf_ * 512:(buf_ + 1) * 512],
              qstage[g_, (4 * seq_ + qj_ // 2), :, (qj_ % 2) * 512:(qj_ % 2 + 1) * 512], sQ[buf_])
        return sQ[buf_].n

    O_BANK = 6
    Z_BANK = 7
    q_ls = [0] * NUNIT
    q_ls[0] = emit_qload(0)
    q_ls[1] = emit_qload(1)
    gb = 0
    u = 0
    for g in range(G):
        if g > 0:
            p.wait("sync", sEV)
        p.dma("sync", maskb[:, :], masks_in[g], sMK)
        mask_ls = sMK.n
        for seq in range(SEQN):
            for qj in range(QJN):
                buf = u % 2
                ntiles = 4 * (qj + 1)
                nbatch = ntiles // 2
                pend = None

                def emit_avz(meta):
                    bi_, eslot_, gb_ = meta
                    p.wait("tensor", sEV, mult_hist[gb_])
                    if bi_ == 0 and u >= 1:
                        p.wait("tensor", sEV, unit_ev[u - 1])
                        p.wait("tensor", sSE, ln_se[u - 1])
                    for t_ in range(2):
                        kc_ = 2 * bi_ + t_
                        pslice = pbf[:, eslot_ * 1024 + t_ * 512: eslot_ * 1024 + (t_ + 1) * 512]
                        p.mm(bank(O_BANK), vt[:, seq * 4096 + kc_ * 128: seq * 4096 + (kc_ + 1) * 128],
                             pslice, start=(bi_ == 0 and t_ == 0),
                             stop=(bi_ == nbatch - 1 and t_ == 1))
                        p.mm(bank(Z_BANK), ones[:, :], pslice,
                             start=(bi_ == 0 and t_ == 0), stop=(bi_ == nbatch - 1 and t_ == 1),
                             sem=(sT2 if t_ == 1 else None))
                    avz_hist.append(sT2.n)

                for bi in range(nbatch):
                    sbase = (gb % 3) * 1024
                    eslot = gb % 3
                    if gb >= 3:
                        p.wait("tensor", sSE, exp_hist[gb - 3])
                    if gb == 0:
                        p.wait("tensor", sEV, bounds_done_ev)
                    p.wait("tensor", sQ[buf], q_ls[u])
                    for t in range(2):
                        kc = 2 * bi + t
                        p.mm(ps[:, sbase + t * 512: sbase + (t + 1) * 512],
                             ksb[:, seq * 4096 + kc * 128: seq * 4096 + (kc + 1) * 128],
                             qsl[:, buf * 512:(buf + 1) * 512], start=True, stop=True,
                             sem=(sMM if t == 1 else None))
                    sc_mm = sMM.n
                    p.wait("scalar", sMM, sc_mm)
                    if gb >= 3:
                        p.wait("scalar", sEV, mult_hist[gb - 3])
                    if gb == 0:
                        p.wait("scalar", sEV, bounds_done_ev)
                    p.act(ebf[:, eslot * 1024: eslot * 1024 + 1024],
                          ps[:, sbase: sbase + 1024], AF.Exp,
                          bias=negb[:, u:u + 1], scale=SCALE, sem=sSE)
                    exp_hist.append(sSE.n)
                    p.wait("vector", sSE, exp_hist[gb])
                    if gb >= 3:
                        p.wait("vector", sT2, avz_hist[gb - 3])
                    if (u == 0 or (g > 0 and seq == 0 and qj == 0)) and bi == 0:
                        p.wait("vector", sMK, mask_ls)
                    for t in range(2):
                        kc = 2 * bi + t
                        off = 512 * qj - 128 * kc + 384
                        p.v("tensor_mul", pbf[:, eslot * 1024 + t * 512: eslot * 1024 + (t + 1) * 512],
                            ebf[:, eslot * 1024 + t * 512: eslot * 1024 + (t + 1) * 512],
                            maskb[:, off:off + 512], sem=(sEV if t == 1 else None))
                    mult_hist.append(sEV.n)
                    if pend is not None:
                        emit_avz(pend)
                    pend = (bi, eslot, gb)
                    gb += 1
                emit_avz(pend)
                tiles_done_mm[u] = sMM.n
                if u + 2 < NUNIT:
                    q_ls[u + 2] = emit_qload(u + 2)
                p.wait("scalar", sT2, avz_hist[gb - 1])
                if u >= 2:
                    p.wait("scalar", sEV, unit_ev[u - 2])
                p.act(zlnb[:, :], bank(Z_BANK), AF.Ln)
                p.act(zrb[:, buf * 512:(buf + 1) * 512], zlnb[:, :], AF.Exp, scale=-1.0, sem=sSE)
                ln_se[u] = sSE.n
                p.wait("vector", sSE, ln_se[u])
                if u >= 2:
                    p.wait("vector", sSA[buf], unit_st[u - 2])
                p.v("tensor_mul", aob[:, buf * 512:(buf + 1) * 512], bank(O_BANK),
                    zrb[:, buf * 512:(buf + 1) * 512], sem=sEV)
                unit_ev[u] = sEV.n
                p.wait("sync", sEV, unit_ev[u])
                p.dma("sync",
                      ag2_in.ap()[g * 128:(g + 1) * 128,
                                  seq * 4096 + qj * 512: seq * 4096 + (qj + 1) * 512],
                      aob[:, buf * 512:(buf + 1) * 512], sSA[buf])
                unit_st[u] = sSA[buf].n
                u += 1

    p.wait("gpsimd", sSA[0])
    p.wait("gpsimd", sSA[1])
    p.collective("AllGather", [list(range(NCORE))], [ag2_in.ap().opt()], [ag2_out.ap().opt()], sCC)
    CC2 = sCC.n

    ag2v3 = ag2_out.ap().rearrange("(r j) t -> r j t", r=NCORE)
    p.wait("sync", sCC, CC2)
    p.wait("sync", sMM, qkv_mm[-1])
    for rblk in range(NCORE):
        p.dma_pid("sync", astage[rblk],
                  lambda pid, rblk=rblk: ag2v3[rblk, :, bass.ts(pid, TOK)], sAS)
    p.wait("sync", sAS)
    astage2 = astage.rearrange("r j t -> (r j) t")
    for uu in range(KC):
        row = 512 * (uu // 4) + 128 * (uu % 4)
        p.dma("sync", xn[:, uu * TOK:(uu + 1) * TOK], astage2[row:row + 128, :], sAT)
    ATTN_ALL = sAT.n

    fo_mm = [0] * KC
    fo_ev = [0] * KC
    fo_st = [0] * KC
    fo_se = [0] * KC
    fo_stats = [0] * KC
    for m in range(KC):
        buf = m % 2
        ob = 6 if buf == 0 else 4
        if m >= 2:
            p.wait("gpsimd", sMM, fo_mm[m - 2])
        elif m == 0:
            p.wait("gpsimd", sMM, qkv_mm[-1])
        p.dma("gpsimd", wst[:, buf * 4096:(buf + 1) * 4096].rearrange("p (k f) -> p k f", k=KC),
              wo_t[m], sW[buf])
        w_lw = sW[buf].n
        if m >= 2:
            p.wait("sync", sEV, fo_ev[m - 2])
        else:
            p.wait("sync", sEV)
        p.wait("sync", sSR[buf], b_store[m])
        p.dma("sync", rin[:, buf * TOK:(buf + 1) * TOK], r1d[m], sR[buf])
        r_ls = sR[buf].n
        p.wait("tensor", sW[buf], w_lw)
        if m == 0:
            p.wait("tensor", sEV, unit_ev[-1])
            p.wait("tensor", sAT, ATTN_ALL)
        if m >= 2:
            p.wait("tensor", sEV, fo_ev[m - 2])
        for k in range(KC):
            for hf in range(2):
                p.mm(bank(ob + hf), wst[:, buf * 4096 + k * 128: buf * 4096 + (k + 1) * 128],
                     xn[:, k * TOK + hf * 512: k * TOK + (hf + 1) * 512],
                     start=(k == 0), stop=(k == KC - 1),
                     sem=(sMM if (k == KC - 1 and hf == 1) else None))
        fo_mm[m] = sMM.n
        p.wait("vector", sMM, fo_mm[m])
        p.wait("vector", sR[buf], r_ls)
        if m >= 2:
            p.wait("vector", sSR[buf], fo_st[m - 2])
            p.wait("vector", sSE, fo_se[m - 2])
        p.v("tensor_add", rosb[:, buf * TOK:(buf + 1) * TOK], bank(ob, 2),
            rin[:, buf * TOK:(buf + 1) * TOK], sem=sEV)
        fo_ev[m] = sEV.n
        p.wait("scalar", sEV, fo_ev[m])
        p.dma("scalar", r2d[m], rosb[:, buf * TOK:(buf + 1) * TOK], sSR[buf])
        fo_st[m] = sSR[buf].n
        if m >= 2:
            p.wait("scalar", sMM, fo_stats[m - 2])
        p.act(rb16[:, buf * TOK:(buf + 1) * TOK], rosb[:, buf * TOK:(buf + 1) * TOK], AF.Copy)
        p.act(sqb[:, buf * TOK:(buf + 1) * TOK], rosb[:, buf * TOK:(buf + 1) * TOK], AF.Square, sem=sSE)
        fo_se[m] = sSE.n

        def emit_fstats(mm_):
            bb_ = mm_ % 2
            p.wait("tensor", sSE, fo_se[mm_])
            for hf in range(2):
                rb = rb16[:, bb_ * TOK + hf * 512: bb_ * TOK + (hf + 1) * 512]
                sq = sqb[:, bb_ * TOK + hf * 512: bb_ * TOK + (hf + 1) * 512]
                p.mm(bank(0 + hf), ones[:, :], rb, start=(mm_ == 0), stop=(mm_ == KC - 1))
                p.mm(bank(2 + hf), ones[:, :], sq, start=(mm_ == 0), stop=(mm_ == KC - 1),
                     sem=(sMM if hf == 1 else None))
            fo_stats[mm_] = sMM.n
        if m >= 1:
            emit_fstats(m - 1)
        if m == KC - 1:
            emit_fstats(m)

    ln_finalize(store_sems=[sSR[0], sSR[1]])
    xn_ev = ln_normalize(r2d, store_gates=fo_st, xn_done_mm=fo_stats[-1],
                         rosb_gate=[(sSR[0], fo_st[KC - 2]), (sSR[1], fo_st[KC - 1])])
    ffn_block(xn_ev, r2d, fo_st, y_out, do_stats=False, w_gate0=fo_mm[-1])

    p.wait("scalar", sSR[0])
    p.wait("scalar", sSR[1])
    p.wait("sync", sSR[0])
    p.wait("sync", sSR[1])

    p.run()
    return nc


_NC_CACHE = [None]


def _prep_inputs(L0_attn_hidden_states, L0_attn_w_qkv, L0_attn_w_o, L0_ffn_W_base,
                 L0_ffn_b_base, L0_ffn_W_gate, L0_ffn_A, L0_ffn_B):
    x = np.asarray(L0_attn_hidden_states, dtype=np.float32)
    w_qkv = np.asarray(L0_attn_w_qkv, dtype=np.float32)
    w_o = np.asarray(L0_attn_w_o, dtype=np.float32)
    W_base = np.asarray(L0_ffn_W_base, dtype=np.float32)
    b_base = np.asarray(L0_ffn_b_base, dtype=np.float32)
    W_gate = np.asarray(L0_ffn_W_gate, dtype=np.float32)
    A = np.asarray(L0_ffn_A, dtype=np.float32)
    Bm = np.asarray(L0_ffn_B, dtype=np.float32)

    wqkv_t = np.ascontiguousarray(w_qkv.reshape(KC, 128, QKVC, 128).transpose(2, 1, 0, 3))
    wo_t = np.ascontiguousarray(w_o.reshape(KC, 128, KC, 128).transpose(2, 1, 0, 3))
    wbase_t = np.ascontiguousarray(W_base.reshape(KC, 128, KC, 128).transpose(0, 3, 2, 1))
    a_t = np.ascontiguousarray(A.reshape(128, KC, 128).transpose(1, 2, 0))
    B_flat = Bm.transpose(0, 2, 1).reshape(128, D)
    b_t = np.ascontiguousarray(B_flat.reshape(128, KC, 128).transpose(1, 0, 2))
    wg_t = np.ascontiguousarray(W_gate.T.reshape(KC, 128, 8))
    bb = np.ascontiguousarray(b_base.reshape(KC, 128).T)
    ones_c = np.ones((128, 128), np.float32)
    ident_c = np.eye(128, dtype=np.float32)
    emat_c = np.zeros((8, 128), np.float32)
    for e in range(8):
        emat_c[e, e * 16:(e + 1) * 16] = 1.0

    slopes = _alibi_slopes(32).reshape(8, 4)
    cc = np.arange(MASKW, dtype=np.int64)[None, :]
    kp = np.arange(128, dtype=np.int64)[:, None]
    rel = cc - 384 - kp
    sqd = np.sqrt(np.maximum(rel, 0).astype(np.float64))
    masks_all = []
    for c in range(NCORE):
        mg = np.empty((G, 128, MASKW), ml_dtypes.bfloat16)
        for g in range(G):
            m = np.exp(-np.float64(slopes[c, g]) * sqd)
            m[rel < 0] = 0.0
            mg[g] = m.astype(ml_dtypes.bfloat16)
        masks_all.append(mg)

    shared = dict(wqkv_t=wqkv_t, wo_t=wo_t, wbase_t=wbase_t, a_t=a_t, b_t=b_t,
                  wg_t=wg_t, bb=bb, ones_c=ones_c, ident_c=ident_c, emat_c=emat_c)
    in_maps = []
    for c in range(NCORE):
        seq, tb = c // 4, c % 4
        xTc = np.ascontiguousarray(
            x[seq, tb * TOK:(tb + 1) * TOK, :].T.reshape(KC, 128, TOK))
        im = dict(shared)
        im["xT"] = xTc
        im["masks"] = masks_all[c]
        in_maps.append(im)
    return in_maps


def kernel(L0_attn_hidden_states, L0_attn_w_qkv, L0_attn_w_o, L0_ffn_W_base,
           L0_ffn_b_base, L0_ffn_W_gate, L0_ffn_A, L0_ffn_B):
    in_maps = _prep_inputs(L0_attn_hidden_states, L0_attn_w_qkv, L0_attn_w_o,
                           L0_ffn_W_base, L0_ffn_b_base, L0_ffn_W_gate, L0_ffn_A, L0_ffn_B)
    if _NC_CACHE[0] is None:
        _NC_CACHE[0] = build_program()
    nc = _NC_CACHE[0]
    res = run_bass_kernel_spmd(nc, in_maps, core_ids=list(range(NCORE)))
    out = np.empty((2, 4096, 4096), np.float32)
    for c in range(NCORE):
        seq, tb = c // 4, c % 4
        y = res.results[c]["y"].reshape(D, TOK)
        out[seq, tb * TOK:(tb + 1) * TOK, :] = y.T
    return out


# version 20
# speedup vs baseline: 1.0569x; 1.0569x over previous
import math
import numpy as np
import ml_dtypes
import concourse.bass as bass
import concourse.mybir as mybir
from concourse.bass_utils import run_bass_kernel_spmd

F32 = mybir.dt.float32
BF16 = mybir.dt.bfloat16
AF = mybir.ActivationFunctionType
ALU = mybir.AluOpType
AX = mybir.AxisListType

D = 4096
TOK = 1024
KC = 32
QKVC = 48
NCORE = 8
G = 4
SEQN = 2
QJN = 8
MASKW = 4480
SCALE = 1.0 / math.sqrt(128.0)
EPS = 1e-5
BIG = 1e30

ENGS = ("sync", "scalar", "vector", "tensor", "gpsimd")


class Sem:
    __slots__ = ("h", "n")

    def __init__(self, nc, name):
        self.h = nc.alloc_semaphore(name)
        self.n = 0


class Prog:
    def __init__(self, nc):
        self.nc = nc
        self.q = {e: [] for e in ENGS}

    def on(self, eng, fn):
        self.q[eng].append(fn)

    def wait(self, eng, sem, val=None):
        v = sem.n if val is None else val
        assert 0 <= v <= sem.n, f"wait on future value {v} > {sem.n}"
        if v <= 0:
            return
        self.q[eng].append(lambda e, h=sem.h, v=v: e.wait_ge(h, v))

    def dma(self, eng, out, in_, sem):
        sem.n += 16
        self.q[eng].append(
            lambda e, o=out, i=in_, h=sem.h: e.dma_start(out=o, in_=i).then_inc(h, 16))

    def dma_pid(self, eng, dst, make_src, sem):
        sem.n += 16

        def f(e, d=dst, ms=make_src, h=sem.h):
            if not hasattr(e, "_pid_cache"):
                e._pid_cache = e.partition_id()
            e.dma_start(out=d, in_=ms(e._pid_cache)).then_inc(h, 16)
        self.q[eng].append(f)

    def mm(self, out, lhsT, rhs, start, stop, sem=None):
        if sem is not None:
            sem.n += 1
        h = sem.h if sem is not None else None

        def f(e, o=out, l=lhsT, r=rhs, s=start, t=stop, h=h):
            ins = e.matmul(o, l, r, start=s, stop=t)
            if h is not None:
                ins.then_inc(h, 1)
        self.q["tensor"].append(f)

    def trans(self, out, in_, ident, sem=None):
        if sem is not None:
            sem.n += 1
        h = sem.h if sem is not None else None

        def f(e, o=out, i=in_, d=ident, h=h):
            ins = e.transpose(o, i, d)
            if h is not None:
                ins.then_inc(h, 1)
        self.q["tensor"].append(f)

    def act(self, out, in_, func, bias=0.0, scale=1.0, sem=None):
        if sem is not None:
            sem.n += 1
        h = sem.h if sem is not None else None

        def f(e, o=out, i=in_, fu=func, b=bias, s=scale, h=h):
            ins = e.activation(out=o, in_=i, func=fu, bias=b, scale=s)
            if h is not None:
                ins.then_inc(h, 1)
        self.q["scalar"].append(f)

    def smul(self, out, in_, c, sem=None):
        if sem is not None:
            sem.n += 1
        h = sem.h if sem is not None else None

        def f(e, o=out, i=in_, c=c, h=h):
            ins = e.mul(o, i, c)
            if h is not None:
                ins.then_inc(h, 1)
        self.q["scalar"].append(f)

    def v(self, opname, *args, sem=None, **kw):
        if sem is not None:
            sem.n += 1
        h = sem.h if sem is not None else None

        def f(e, opname=opname, args=args, kw=kw, h=h):
            ins = getattr(e, opname)(*args, **kw)
            if h is not None:
                ins.then_inc(h, 1)
        self.q["vector"].append(f)

    def collective(self, kind, replica_groups, ins, outs, sem):
        sem.n += 1

        def f(e, k=kind, rg=replica_groups, i=ins, o=outs, h=sem.h):
            e.collective_compute(k, ALU.bypass, replica_groups=rg, ins=i, outs=o).then_inc(h, 1)
        self.q["gpsimd"].append(f)

    def run(self):
        with self.nc.Block() as block:
            for eng in ENGS:
                fns = self.q[eng]
                getattr(block, eng)(lambda e, fns=fns: [fn(e) for fn in fns])


def _alibi_slopes(n):
    cp2 = 2 ** math.floor(math.log2(n))
    base = 2.0 ** (-8.0 / cp2)
    slopes = base ** np.arange(1, cp2 + 1, dtype=np.float32)
    if cp2 != n:
        extra_base = 2.0 ** (-4.0 / cp2)
        extra = extra_base ** np.arange(1, 1 + 2 * (n - cp2), 2, dtype=np.float32)
        slopes = np.concatenate([slopes, extra])
    return slopes.astype(np.float32)


def build_program():
    nc = bass.Bass("TRN2", target_bir_lowering=False)

    xT = nc.dram_tensor("xT", [KC, 128, TOK], F32, kind="ExternalInput").ap()
    wqkv_t = nc.dram_tensor("wqkv_t", [QKVC, 128, KC, 128], F32, kind="ExternalInput").ap()
    wo_t = nc.dram_tensor("wo_t", [KC, 128, KC, 128], F32, kind="ExternalInput").ap()
    wbase_t = nc.dram_tensor("wbase_t", [KC, 128, KC, 128], F32, kind="ExternalInput").ap()
    a_t_in = nc.dram_tensor("a_t", [KC, 128, 128], F32, kind="ExternalInput").ap()
    b_t_in = nc.dram_tensor("b_t", [KC, 128, 128], F32, kind="ExternalInput").ap()
    wg_in = nc.dram_tensor("wg_t", [KC, 128, 8], F32, kind="ExternalInput").ap()
    bb_in = nc.dram_tensor("bb", [128, KC], F32, kind="ExternalInput").ap()
    masks_in = nc.dram_tensor("masks", [G, 128, MASKW], BF16, kind="ExternalInput").ap()
    ones_in = nc.dram_tensor("ones_c", [128, 128], F32, kind="ExternalInput").ap()
    ident_in = nc.dram_tensor("ident_c", [128, 128], F32, kind="ExternalInput").ap()
    emat_in = nc.dram_tensor("emat_c", [8, 128], F32, kind="ExternalInput").ap()

    y_out = nc.dram_tensor("y", [KC, 128, TOK], F32, kind="ExternalOutput").ap()

    r1d = nc.dram_tensor("r1d", [KC, 128, TOK], F32).ap()
    r2d = nc.dram_tensor("r2d", [KC, 128, TOK], F32).ap()
    ag1_in = nc.dram_tensor("ag1_in", [QKVC * 128, TOK], BF16)
    ag1_out = nc.dram_tensor("ag1_out", [NCORE * QKVC * 128, TOK], BF16, addr_space="Shared")
    ag2_in = nc.dram_tensor("ag2_in", [512, NCORE * TOK], BF16)
    ag2_out = nc.dram_tensor("ag2_out", [NCORE * 512, NCORE * TOK], BF16, addr_space="Shared")
    qstage = nc.dram_tensor("qstage", [G, NCORE, 128, TOK], BF16).ap()
    astage = nc.dram_tensor("astage", [NCORE, 512, TOK], BF16).ap()

    sb = nc.alloc_sbuf_tensor
    xn = sb("xn", [128, KC * TOK], BF16)[:]
    wst = sb("wst", [128, 2 * 4096], BF16)[:]
    ksb = sb("ksb", [128, 2 * 4096], BF16)[:]
    vt = sb("vt", [128, 2 * 4096], BF16)[:]
    maskb = sb("maskb", [128, MASKW], BF16)[:]
    a_t = sb("a_tb", [128, KC * 128], BF16)[:]
    b_t = sb("b_tb", [128, KC * 128], BF16)[:]
    wg_t = sb("wg_tb", [128, KC * 8], BF16)[:]
    hsb = sb("hsb", [128, TOK], F32)[:]
    hw = sb("hw", [128, TOK], BF16)[:]
    ltok = sb("ltok", [128, 64], F32)[:]
    m1b = sb("m1b", [128, 8], F32)[:]
    m2b = sb("m2b", [128, 8], F32)[:]
    tm1 = sb("tm1", [128, 64], F32)[:]
    tm2 = sb("tm2", [128, 64], F32)[:]
    l2b = sb("l2b", [128, 64], F32)[:]
    db_ = l2b
    emb = sb("emb", [128, 64], F32)[:]
    edb = tm2
    zb8 = sb("zb8", [128, 8], F32)[:]
    zr8 = sb("zr8", [128, 8], F32)[:]
    rwb = sb("rwb", [128, 64], BF16)[:]
    rwT = sb("rwT", [8, TOK], BF16)[:]
    mstat = sb("mstat", [128, TOK], F32)[:]
    rstd = sb("rstd", [128, TOK], F32)[:]
    rin = sb("rin", [128, 2 * TOK], F32)[:]
    rosb = sb("rosb", [128, 2 * TOK], F32)[:]
    rb16 = sb("rb16", [128, 2 * TOK], BF16)[:]
    sqb = sb("sqb", [128, 2 * TOK], BF16)[:]
    qsl = sb("qsl", [128, 2 * 512], BF16)[:]
    q2b = sb("q2b", [128, 2 * 512], BF16)[:]
    esb = sb("esb", [128, 2 * 512], F32)[:]
    pbf = sb("pbf", [128, 2 * 1536], BF16)[:]
    ebf = sb("ebf", [128, 2 * 1536], BF16)[:]
    zrb = sb("zrb", [128, 2 * 512], F32)[:]
    zlnb = sb("zlnb", [128, 512], F32)[:]
    aob = sb("aob", [128, 2 * 512], BF16)[:]
    negb = mstat[:, 0:64]
    bq1 = sb("bq1", [128, 4], F32)[:]
    k2maxs = sb("k2maxs", [128, 2], F32)[:]
    k2cols = sb("k2cols", [128, 16], F32)[:]
    k2max = sb("k2max", [128, 2], F32)[:]
    ones = sb("ones", [128, 128], BF16)[:]
    identb = sb("identb", [128, 128], BF16)[:]
    identf = sb("identf", [128, 8], F32)[:]
    emat = sb("emat", [8, 128], BF16)[:]
    halfb = sb("halfb", [128, KC], F32)[:]
    scr8k = wst[:, 0:4096]
    gsb = esb[0:8, :]

    ps = nc.alloc_psum_tensor("ps", [128, 8 * 512], F32)[:]

    def bank(i, w=1):
        return ps[:, i * 512:(i + w) * 512]

    p = Prog(nc)
    sMM = Sem(nc, "sMM")
    sT2 = Sem(nc, "sT2")
    sEV = Sem(nc, "sEV")
    sSE = Sem(nc, "sSE")
    sCC = Sem(nc, "sCC")
    sW = [Sem(nc, "sW0"), Sem(nc, "sW1")]
    sR = [Sem(nc, "sR0"), Sem(nc, "sR1")]
    sXB = [Sem(nc, "sXB0"), Sem(nc, "sXB1")]
    sQ = [Sem(nc, "sQ0"), Sem(nc, "sQ1")]
    sSR = [Sem(nc, "sSR0"), Sem(nc, "sSR1")]
    sSQ = [Sem(nc, "sSQ0"), Sem(nc, "sSQ1")]
    sSA = [Sem(nc, "sSA0"), Sem(nc, "sSA1")]
    sC = Sem(nc, "sC")
    sCG = Sem(nc, "sCG")
    sK = Sem(nc, "sK")
    sV = Sem(nc, "sV")
    sQS = Sem(nc, "sQS")
    sMK = Sem(nc, "sMK")
    sAT = Sem(nc, "sAT")
    sAS = Sem(nc, "sAS")

    p.dma("gpsimd", ones[:, :], ones_in, sCG)
    p.dma("gpsimd", identb[:, :], ident_in, sCG)
    p.dma("gpsimd", emat[:, :], emat_in, sCG)
    p.dma("gpsimd", a_t[:].rearrange("p (k f) -> p k f", k=KC), a_t_in.transpose([1, 0, 2]), sCG)
    p.dma("gpsimd", b_t[:].rearrange("p (k f) -> p k f", k=KC), b_t_in.transpose([1, 0, 2]), sCG)
    p.dma("gpsimd", wg_t[:].rearrange("p (k f) -> p k f", k=KC), wg_in.transpose([1, 0, 2]), sCG)
    p.dma("sync", identf[:, :], ident_in[:, 0:8], sC)
    p.dma("sync", esb[:, 0:KC], bb_in, sC)
    p.wait("vector", sC)
    p.v("tensor_scalar_mul", halfb[:, :], esb[:, 0:KC], 0.5, sem=sEV)
    HALFB_EV = sEV.n

    statmm_done = []
    for k in range(KC):
        buf = k % 2
        if k >= 2:
            p.wait("gpsimd", sMM, statmm_done[k - 2])
        p.dma("gpsimd", rb16[:, buf * TOK:(buf + 1) * TOK], xT[k], sXB[buf])
        lw_here = sXB[buf].n
        p.wait("scalar", sXB[buf], lw_here)
        if k >= 2:
            p.wait("scalar", sMM, statmm_done[k - 2])
        p.act(sqb[:, buf * TOK:(buf + 1) * TOK], rb16[:, buf * TOK:(buf + 1) * TOK], AF.Square, sem=sSE)
        se_here = sSE.n
        if k == 0:
            p.wait("tensor", sCG)
        p.wait("tensor", sXB[buf], lw_here)
        p.wait("tensor", sSE, se_here)
        for hf in range(2):
            rb = rb16[:, buf * TOK + hf * 512: buf * TOK + (hf + 1) * 512]
            sq = sqb[:, buf * TOK + hf * 512: buf * TOK + (hf + 1) * 512]
            p.mm(bank(0 + hf), ones[:, :], rb, start=(k == 0), stop=(k == KC - 1))
            p.mm(bank(2 + hf), ones[:, :], sq, start=(k == 0), stop=(k == KC - 1),
                 sem=(sMM if hf == 1 else None))
        statmm_done.append(sMM.n)

    state = {"rstd_se": 0}

    def ln_finalize(store_sems=None):
        T1 = esb[:, 0:512 * 2]
        T2 = rosb[:, 0:TOK]
        p.wait("vector", sMM)
        if store_sems is not None:
            for s_ in store_sems:
                p.wait("vector", s_)
        p.v("tensor_scalar_mul", mstat[:, :], bank(0, 2), 1.0 / D, sem=sEV)
        p.wait("vector", sEV)
        p.v("tensor_mul", T1, mstat[:, :], mstat[:, :], sem=sEV)
        p.wait("vector", sEV)
        p.v("scalar_tensor_tensor", T2, bank(2, 2), 1.0 / D, T1,
            op0=ALU.mult, op1=ALU.subtract, sem=sEV)
        p.wait("vector", sEV)
        p.v("tensor_scalar_add", T1, T2, EPS, sem=sEV)
        p.wait("vector", sEV)
        p.v("reciprocal", T2, T1, sem=sEV)
        p.wait("scalar", sEV)
        p.act(rstd[:, :], T2, AF.Sqrt, sem=sSE)
        state["rstd_se"] = sSE.n

    ln_finalize()

    def ln_normalize(src_dram, store_gates=None, xn_done_mm=None, rosb_gate=None):
        ev_done = []

        def T(k):
            return esb[:, 0:TOK] if k % 2 == 0 else rosb[:, 0:TOK]

        def emit_mult(k):
            p.v("tensor_mul", xn[:, k * TOK:(k + 1) * TOK], T(k), rstd[:, :], sem=sEV)
            ev_done.append(sEV.n)

        for k in range(KC):
            buf = k % 2
            lq = "sync" if buf == 0 else "scalar"
            if k >= 2:
                p.wait(lq, sEV, ev_done[k - 2])
            else:
                p.wait(lq, sEV)
            if store_gates is not None:
                p.wait(lq, sSR[k % 2], store_gates[k])
            p.dma(lq, rin[:, buf * TOK:(buf + 1) * TOK], src_dram[k], sR[buf])
            ls_here = sR[buf].n
            p.wait("vector", sR[buf], ls_here)
            if k == 0:
                p.wait("vector", sSE, state["rstd_se"])
                if xn_done_mm is not None:
                    p.wait("vector", sMM, xn_done_mm)
            if k == 1 and rosb_gate is not None:
                for s_, v_ in rosb_gate:
                    p.wait("vector", s_, v_)
            p.v("tensor_sub", T(k), rin[:, buf * TOK:(buf + 1) * TOK], mstat[:, :])
            if k >= 1:
                emit_mult(k - 1)
        emit_mult(KC - 1)
        return ev_done

    xn_ev = ln_normalize(xT)

    def ffn_block(xn_ev, r_src_dram, r_gates, out_dram, do_stats, w_gate0=0):
        gate_ev = 0
        for hf in range(2):
            for k in range(KC):
                if hf == 0:
                    p.wait("tensor", sEV, xn_ev[k])
                if k == 0 and hf == 1:
                    p.wait("tensor", sEV, gate_ev)
                xk = xn[:, k * TOK + hf * 512: k * TOK + (hf + 1) * 512]
                p.mm(ps[0:8, 4 * 512:4 * 512 + 512], wg_t[:, k * 8:(k + 1) * 8], xk,
                     start=(k == 0), stop=(k == KC - 1))
                p.mm(bank(5), a_t[:, k * 128:(k + 1) * 128], xk,
                     start=(k == 0), stop=(k == KC - 1), sem=(sMM if k == KC - 1 else None))
            mm_here = sMM.n
            p.wait("vector", sMM, mm_here)
            p.v("tensor_copy", gsb[0:8, hf * 512:(hf + 1) * 512], ps[0:8, 4 * 512:4 * 512 + 512], sem=sEV)
            p.wait("vector", sEV)
            p.v("tensor_copy", hsb[:, hf * 512:(hf + 1) * 512], bank(5), sem=sEV)
            p.wait("vector", sEV)
            gate_ev = sEV.n

        p.wait("tensor", sEV, gate_ev)
        for t in range(8):
            p.trans(ps[:, 6 * 512 + t * 8: 6 * 512 + (t + 1) * 8], gsb[0:8, t * 128:(t + 1) * 128],
                    identf[0:8, 0:8], sem=(sMM if t == 7 else None))
        p.wait("vector", sMM)
        p.v("tensor_copy", ltok[:, :], ps[:, 6 * 512:6 * 512 + 64], sem=sEV)
        p.wait("vector", sEV)

        def l3(apx):
            return apx.rearrange("p (b e) -> p b e", b=8)

        def bc(ap8):
            return ap8.unsqueeze(2).broadcast_to([128, 8, 8])

        p.v("reduce_max", m1b[:, :].unsqueeze(2), l3(ltok[:, :]), axis=AX.X, sem=sEV)
        p.wait("vector", sEV)
        p.v("tensor_tensor", l3(tm1[:, :]), l3(ltok[:, :]), bc(m1b[:, :]), op=ALU.is_equal, sem=sEV)
        p.wait("vector", sEV)
        p.v("scalar_tensor_tensor", l2b[:, :], tm1[:, :], -BIG, ltok[:, :],
            op0=ALU.mult, op1=ALU.add, sem=sEV)
        p.wait("vector", sEV)
        p.v("reduce_max", m2b[:, :].unsqueeze(2), l3(l2b[:, :]), axis=AX.X, sem=sEV)
        p.wait("vector", sEV)
        p.v("tensor_tensor", l3(tm2[:, :]), l3(l2b[:, :]), bc(m2b[:, :]), op=ALU.is_equal, sem=sEV)
        p.wait("vector", sEV)
        p.v("tensor_add", tm1[:, :], tm1[:, :], tm2[:, :], sem=sEV)
        p.wait("vector", sEV)
        p.v("tensor_tensor", l3(db_[:, :]), l3(ltok[:, :]), bc(m1b[:, :]), op=ALU.subtract, sem=sEV)
        p.wait("scalar", sEV)
        p.act(edb[:, :], db_[:, :], AF.Exp, sem=sSE)
        p.wait("vector", sSE)
        p.v("tensor_mul", emb[:, :], edb[:, :], tm1[:, :], sem=sEV)
        p.wait("vector", sEV)
        p.v("reduce_sum", zb8[:, :].unsqueeze(2), l3(emb[:, :]), axis=AX.X, sem=sEV)
        p.wait("vector", sEV)
        p.v("reciprocal", zr8[:, :], zb8[:, :], sem=sEV)
        p.wait("vector", sEV)
        p.v("tensor_tensor", l3(rwb[:, :]), l3(emb[:, :]), bc(zr8[:, :]), op=ALU.mult, sem=sEV)
        rw_ev = sEV.n
        p.wait("tensor", sEV, rw_ev)
        rwT_ps = ps[0:8, 6 * 512:7 * 512].bitcast(BF16)
        for t in range(8):
            p.trans(rwT_ps[:, t * 128:(t + 1) * 128], rwb[:, t * 8:(t + 1) * 8], identb[:, :],
                    sem=(sMM if t == 7 else None))
        p.wait("vector", sMM)
        p.v("tensor_copy", rwT[0:8, :], rwT_ps, sem=sEV)
        p.wait("vector", sEV)
        p.wait("tensor", sEV)
        p.mm(bank(6), emat[0:8, :], rwT[0:8, 0:512], start=True, stop=True)
        p.mm(bank(7), emat[0:8, :], rwT[0:8, 512:1024], start=True, stop=True, sem=sMM)
        p.wait("vector", sMM)
        p.wait("vector", sEV, HALFB_EV)
        p.v("tensor_mul", hw[:, 0:512], hsb[:, 0:512], bank(6))
        p.v("tensor_mul", hw[:, 512:1024], hsb[:, 512:1024], bank(7), sem=sEV)
        hw_ev = sEV.n

        evac_ev = [0] * KC
        store_st = [0] * KC
        se_cp = [0] * KC
        stats_mm = [0] * KC
        mm_done = [0] * KC
        for m in range(KC):
            buf = m % 2
            ob = 6 if buf == 0 else 4
            if m >= 2:
                p.wait("gpsimd", sMM, mm_done[m - 2])
            elif m == 0:
                p.wait("gpsimd", sMM, w_gate0)
            p.dma("gpsimd", wst[:, buf * 4096:(buf + 1) * 4096].rearrange("p (k f) -> p k f", k=KC),
                  wbase_t[m], sW[buf])
            w_lw = sW[buf].n
            if m >= 2:
                p.wait("sync", sEV, evac_ev[m - 2])
            else:
                p.wait("sync", sEV)
            if r_gates is not None:
                p.wait("sync", sSR[buf], r_gates[m])
            p.dma("sync", rin[:, buf * TOK:(buf + 1) * TOK], r_src_dram[m], sR[buf])
            r_ls = sR[buf].n
            p.wait("tensor", sW[buf], w_lw)
            if m >= 2:
                p.wait("tensor", sEV, evac_ev[m - 2])
            if m <= 1:
                p.wait("tensor", sEV, hw_ev)
            for k in range(KC):
                for hf in range(2):
                    p.mm(bank(ob + hf), wst[:, buf * 4096 + k * 128: buf * 4096 + (k + 1) * 128],
                         xn[:, k * TOK + hf * 512: k * TOK + (hf + 1) * 512],
                         start=(k == 0), stop=False)
            for hf in range(2):
                p.mm(bank(ob + hf), b_t[:, m * 128:(m + 1) * 128], hw[:, hf * 512:(hf + 1) * 512],
                     start=False, stop=True, sem=(sMM if hf == 1 else None))
            mm_done[m] = sMM.n
            p.wait("vector", sMM, mm_done[m])
            p.wait("vector", sR[buf], r_ls)
            if m >= 2:
                p.wait("vector", sSR[buf], store_st[m - 2])
                if do_stats:
                    p.wait("vector", sSE, se_cp[m - 2])
            p.v("tensor_scalar", esb[:, 0:TOK], bank(ob, 2), 0.5, halfb[:, m:m + 1],
                op0=ALU.mult, op1=ALU.add)
            p.v("tensor_add", rosb[:, buf * TOK:(buf + 1) * TOK], esb[:, 0:TOK],
                rin[:, buf * TOK:(buf + 1) * TOK], sem=sEV)
            evac_ev[m] = sEV.n
            p.wait("scalar", sEV, evac_ev[m])
            p.dma("scalar", out_dram[m], rosb[:, buf * TOK:(buf + 1) * TOK], sSR[buf])
            store_st[m] = sSR[buf].n
            if do_stats:
                if m >= 2:
                    p.wait("scalar", sMM, stats_mm[m - 2])
                p.act(rb16[:, buf * TOK:(buf + 1) * TOK], rosb[:, buf * TOK:(buf + 1) * TOK], AF.Copy)
                p.act(sqb[:, buf * TOK:(buf + 1) * TOK], rosb[:, buf * TOK:(buf + 1) * TOK],
                      AF.Square, sem=sSE)
                se_cp[m] = sSE.n

                def emit_stats(mm_):
                    bb_ = mm_ % 2
                    p.wait("tensor", sSE, se_cp[mm_])
                    for hf in range(2):
                        rb = rb16[:, bb_ * TOK + hf * 512: bb_ * TOK + (hf + 1) * 512]
                        sq = sqb[:, bb_ * TOK + hf * 512: bb_ * TOK + (hf + 1) * 512]
                        p.mm(bank(0 + hf), ones[:, :], rb, start=(mm_ == 0), stop=(mm_ == KC - 1))
                        p.mm(bank(2 + hf), ones[:, :], sq, start=(mm_ == 0), stop=(mm_ == KC - 1),
                             sem=(sMM if hf == 1 else None))
                    stats_mm[mm_] = sMM.n
                if m >= 1:
                    emit_stats(m - 1)
                if m == KC - 1:
                    emit_stats(m)
        return evac_ev, store_st, mm_done, stats_mm

    _, b_store, b_mm, b_stats = ffn_block(xn_ev, xT, None, r1d, do_stats=True)

    ln_finalize(store_sems=[sSR[0], sSR[1]])
    xn_ev = ln_normalize(r1d, store_gates=b_store, xn_done_mm=b_mm[-1],
                         rosb_gate=[(sSR[0], b_store[KC - 2]), (sSR[1], b_store[KC - 1])])

    qkv_mm = [0] * QKVC
    qkv_ev = [0] * QKVC
    qkv_st = [0] * QKVC
    for mi in range(QKVC):
        m = mi
        buf = mi % 2
        ob = 0 if buf == 0 else 2
        if mi >= 2:
            p.wait("gpsimd", sMM, qkv_mm[mi - 2])
        elif mi == 0:
            p.wait("gpsimd", sMM, b_mm[-1])
        p.dma("gpsimd", wst[:, buf * 4096:(buf + 1) * 4096].rearrange("p (k f) -> p k f", k=KC),
              wqkv_t[m], sW[buf])
        w_lw = sW[buf].n
        p.wait("tensor", sW[buf], w_lw)
        if mi >= 2:
            p.wait("tensor", sEV, qkv_ev[mi - 2])
        for k in range(KC):
            if mi == 0:
                p.wait("tensor", sEV, xn_ev[k])
            for hf in range(2):
                p.mm(bank(ob + hf), wst[:, buf * 4096 + k * 128: buf * 4096 + (k + 1) * 128],
                     xn[:, k * TOK + hf * 512: k * TOK + (hf + 1) * 512],
                     start=(k == 0), stop=(k == KC - 1),
                     sem=(sMM if (k == KC - 1 and hf == 1) else None))
        qkv_mm[mi] = sMM.n
        p.wait("vector", sMM, qkv_mm[mi])
        if mi == 0:
            p.wait("vector", sSE)
        if mi >= 2:
            p.wait("vector", sSQ[buf], qkv_st[mi - 2])
        p.v("tensor_copy", rb16[:, buf * TOK:(buf + 1) * TOK], bank(ob, 2), sem=sEV)
        qkv_ev[mi] = sEV.n
        p.wait("scalar", sEV, qkv_ev[mi])
        p.dma("scalar", ag1_in.ap()[m * 128:(m + 1) * 128, :], rb16[:, buf * TOK:(buf + 1) * TOK],
              sSQ[buf])
        qkv_st[mi] = sSQ[buf].n

    p.wait("gpsimd", sSQ[0])
    p.wait("gpsimd", sSQ[1])
    p.collective("AllGather", [list(range(NCORE))], [ag1_in.ap().opt()], [ag1_out.ap().opt()], sCC)
    CC_kv = CC_q = sCC.n

    agv3 = ag1_out.ap().rearrange("(r x) t -> r x t", r=NCORE)
    p.wait("sync", sCC, CC_kv)
    p.dma_pid("sync", ksb[:].rearrange("p (r t) -> p r t", r=8),
              lambda pid: agv3[:, bass.ts(32 + pid, 128), :].transpose([1, 0, 2]), sK)

    vt_ev = []
    k2_ev = []
    for seq in range(SEQN):
        if seq == 1:
            p.wait("sync", sMM)
        p.dma_pid("sync", scr8k[:].rearrange("p (r t) -> p r t", r=4),
                  lambda pid, seq=seq: agv3[4 * seq:4 * seq + 4,
                                            bass.ts(40 + pid, 128), :].transpose([1, 0, 2]), sV)
        v_ls = sV.n
        vwin = ps[:, 7 * 512:8 * 512].bitcast(BF16)
        p.wait("tensor", sV, v_ls)
        for grp in range(4):
            if grp > 0 or seq > 0:
                p.wait("tensor", sEV)
            for t in range(8):
                kc = grp * 8 + t
                p.trans(vwin[:, t * 128:(t + 1) * 128], scr8k[:, kc * 128:(kc + 1) * 128],
                        identb[:, :], sem=(sMM if t == 7 else None))
            p.wait("vector", sMM)
            p.v("tensor_copy", vt[:, seq * 4096 + grp * 1024: seq * 4096 + (grp + 1) * 1024],
                vwin, sem=sEV)
            p.wait("vector", sEV)
        vt_ev.append(sEV.n)
        p.wait("scalar", sMM)
        p.wait("scalar", sK)
        p.act(scr8k[:, :], ksb[:, seq * 4096:(seq + 1) * 4096], AF.Square, sem=sSE)
        k2sq = sSE.n
        p.wait("tensor", sSE, k2sq)
        for s8 in range(8):
            if s8 > 0 or seq > 0:
                p.wait("tensor", sEV)
            p.mm(bank(6), ones[:, :], scr8k[:, s8 * 512:(s8 + 1) * 512], start=True, stop=True, sem=sMM)
            p.wait("vector", sMM)
            p.v("reduce_max", k2cols[:, seq * 8 + s8: seq * 8 + s8 + 1].unsqueeze(2),
                bank(6).unsqueeze(1), axis=AX.X, sem=sEV)
            p.wait("vector", sEV)
        p.v("reduce_max", k2max[:, seq:seq + 1].unsqueeze(2),
            k2cols[:, seq * 8:(seq + 1) * 8].unsqueeze(1), axis=AX.X, sem=sEV)
        p.wait("vector", sEV)
        k2_ev.append(sEV.n)

    NUNIT = G * SEQN * QJN

    def unit_coords(uu):
        g_, r_ = divmod(uu, SEQN * QJN)
        seq_, qj_ = divmod(r_, QJN)
        return g_, seq_, qj_

    p.wait("sync", sCC, CC_q)
    for g in range(G):
        p.dma_pid("sync", qstage[g],
                  lambda pid, g=g: agv3[:, bass.ts(4 * pid + g, 128), :], sQS)

    p.wait("vector", sEV, k2_ev[1])
    p.v("tensor_scalar_mul", k2maxs[:, :], k2max[:, :], -SCALE / 2.0, sem=sEV)
    p.wait("vector", sEV)
    p.wait("sync", sQS, 16 * G)
    bq_loads = [0] * NUNIT
    q2_se = [0] * NUNIT
    bmm = [0] * NUNIT
    bq_ev = [0] * NUNIT
    for b in range(NUNIT):
        g_, seq_, qj_ = unit_coords(b)
        buf_ = b % 2
        if b >= 2:
            p.wait("sync", sSE, q2_se[b - 2])
        p.dma("sync", qsl[:, buf_ * 512:(buf_ + 1) * 512],
              qstage[g_, (4 * seq_ + qj_ // 2), :, (qj_ % 2) * 512:(qj_ % 2 + 1) * 512], sQ[buf_])
        bq_loads[b] = sQ[buf_].n
        p.wait("scalar", sQ[buf_], bq_loads[b])
        if b >= 2:
            p.wait("scalar", sMM, bmm[b - 2])
        p.act(q2b[:, buf_ * 512:(buf_ + 1) * 512], qsl[:, buf_ * 512:(buf_ + 1) * 512],
              AF.Square, sem=sSE)
        q2_se[b] = sSE.n
        p.wait("tensor", sSE, q2_se[b])
        if b < 2:
            p.wait("tensor", sEV, qkv_ev[-1])
        else:
            p.wait("tensor", sEV, bq_ev[b - 2])
        p.mm(bank(b % 2), ones[:, :], q2b[:, buf_ * 512:(buf_ + 1) * 512],
             start=True, stop=True, sem=sMM)
        bmm[b] = sMM.n
        p.wait("vector", sMM, bmm[b])
        p.v("reduce_max", bq1[:, (b % 4):(b % 4) + 1].unsqueeze(2), bank(b % 2).unsqueeze(1),
            axis=AX.X, sem=sEV)
        p.wait("vector", sEV)
        p.v("scalar_tensor_tensor", negb[:, b:b + 1], bq1[:, (b % 4):(b % 4) + 1], -SCALE / 2.0,
            k2maxs[:, seq_:seq_ + 1], op0=ALU.mult, op1=ALU.add, sem=sEV)
        bq_ev[b] = sEV.n
        p.wait("vector", sEV)
    bounds_done_ev = sEV.n

    tiles_done_mm = [0] * NUNIT
    unit_ev = [0] * NUNIT
    unit_st = [0] * NUNIT
    ln_se = [0] * NUNIT
    exp_hist = []
    mult_hist = []
    avz_hist = []

    def emit_qload(uu):
        g_, seq_, qj_ = unit_coords(uu)
        buf_ = uu % 2
        if uu >= 2:
            p.wait("sync", sMM, tiles_done_mm[uu - 2])
        else:
            p.wait("sync", sSE, q2_se[NUNIT - 2 + uu])
        p.dma("sync", qsl[:, buf_ * 512:(buf_ + 1) * 512],
              qstage[g_, (4 * seq_ + qj_ // 2), :, (qj_ % 2) * 512:(qj_ % 2 + 1) * 512], sQ[buf_])
        return sQ[buf_].n

    O_BANK = 6
    Z_BANK = 7
    q_ls = [0] * NUNIT
    q_ls[0] = emit_qload(0)
    q_ls[1] = emit_qload(1)

    def emit_avz(meta):
        bi_, eslot_, gb_, u_, seq_, nb_ = meta
        p.wait("tensor", sEV, mult_hist[gb_])
        if bi_ == 0 and u_ >= 1:
            p.wait("tensor", sEV, unit_ev[u_ - 1])
            p.wait("tensor", sSE, ln_se[u_ - 1])
        for t_ in range(2):
            kc_ = 2 * bi_ + t_
            pslice = pbf[:, eslot_ * 1024 + t_ * 512: eslot_ * 1024 + (t_ + 1) * 512]
            p.mm(bank(O_BANK), vt[:, seq_ * 4096 + kc_ * 128: seq_ * 4096 + (kc_ + 1) * 128],
                 pslice, start=(bi_ == 0 and t_ == 0),
                 stop=(bi_ == nb_ - 1 and t_ == 1))
            p.mm(bank(Z_BANK), ones[:, :], pslice,
                 start=(bi_ == 0 and t_ == 0), stop=(bi_ == nb_ - 1 and t_ == 1),
                 sem=(sT2 if t_ == 1 else None))
        avz_hist.append(sT2.n)

    def emit_evac(uu):
        g2, seq2, qj2 = unit_coords(uu)
        buf2 = uu % 2
        p.wait("scalar", sT2, avz_hist[-1])
        if uu >= 2:
            p.wait("scalar", sEV, unit_ev[uu - 2])
        p.act(zlnb[:, :], bank(Z_BANK), AF.Ln)
        p.act(zrb[:, buf2 * 512:(buf2 + 1) * 512], zlnb[:, :], AF.Exp, scale=-1.0, sem=sSE)
        ln_se[uu] = sSE.n
        p.wait("vector", sSE, ln_se[uu])
        if uu >= 2:
            p.wait("vector", sSA[buf2], unit_st[uu - 2])
        p.v("tensor_mul", aob[:, buf2 * 512:(buf2 + 1) * 512], bank(O_BANK),
            zrb[:, buf2 * 512:(buf2 + 1) * 512], sem=sEV)
        unit_ev[uu] = sEV.n
        p.wait("sync", sEV, unit_ev[uu])
        p.dma("sync",
              ag2_in.ap()[g2 * 128:(g2 + 1) * 128,
                          seq2 * 4096 + qj2 * 512: seq2 * 4096 + (qj2 + 1) * 512],
              aob[:, buf2 * 512:(buf2 + 1) * 512], sSA[buf2])
        unit_st[uu] = sSA[buf2].n

    def process_pend(pd):
        emit_avz(pd)
        if pd[0] == pd[5] - 1:
            emit_evac(pd[3])

    gb = 0
    u = 0
    pq = []
    for g in range(G):
        if g > 0:
            p.wait("sync", sEV)
        p.dma("sync", maskb[:, :], masks_in[g], sMK)
        mask_ls = sMK.n
        for seq in range(SEQN):
            for qj in range(QJN):
                buf = u % 2
                ntiles = 4 * (qj + 1)
                nbatch = ntiles // 2
                for bi in range(nbatch):
                    sbase = (gb % 3) * 1024
                    eslot = gb % 3
                    if gb >= 3:
                        p.wait("tensor", sSE, exp_hist[gb - 3])
                    if gb == 0:
                        p.wait("tensor", sEV, bounds_done_ev)
                    p.wait("tensor", sQ[buf], q_ls[u])
                    for t in range(2):
                        kc = 2 * bi + t
                        p.mm(ps[:, sbase + t * 512: sbase + (t + 1) * 512],
                             ksb[:, seq * 4096 + kc * 128: seq * 4096 + (kc + 1) * 128],
                             qsl[:, buf * 512:(buf + 1) * 512], start=True, stop=True,
                             sem=(sMM if t == 1 else None))
                    sc_mm = sMM.n
                    p.wait("scalar", sMM, sc_mm)
                    if gb >= 3:
                        p.wait("scalar", sEV, mult_hist[gb - 3])
                    if gb == 0:
                        p.wait("scalar", sEV, bounds_done_ev)
                    p.act(ebf[:, eslot * 1024: eslot * 1024 + 1024],
                          ps[:, sbase: sbase + 1024], AF.Exp,
                          bias=negb[:, u:u + 1], scale=SCALE, sem=sSE)
                    exp_hist.append(sSE.n)
                    p.wait("vector", sSE, exp_hist[gb])
                    if gb >= 3:
                        p.wait("vector", sT2, avz_hist[gb - 3])
                    if (u == 0 or (g > 0 and seq == 0 and qj == 0)) and bi == 0:
                        p.wait("vector", sMK, mask_ls)
                    for t in range(2):
                        kc = 2 * bi + t
                        off = 512 * qj - 128 * kc + 384
                        p.v("tensor_mul", pbf[:, eslot * 1024 + t * 512: eslot * 1024 + (t + 1) * 512],
                            ebf[:, eslot * 1024 + t * 512: eslot * 1024 + (t + 1) * 512],
                            maskb[:, off:off + 512], sem=(sEV if t == 1 else None))
                    mult_hist.append(sEV.n)
                    if len(pq) == 2:
                        process_pend(pq.pop(0))
                    pq.append((bi, eslot, gb, u, seq, nbatch))
                    gb += 1
                tiles_done_mm[u] = sMM.n
                if u + 2 < NUNIT:
                    q_ls[u + 2] = emit_qload(u + 2)
                u += 1
    for pd in pq:
        process_pend(pd)

    p.wait("gpsimd", sSA[0])
    p.wait("gpsimd", sSA[1])
    p.collective("AllGather", [list(range(NCORE))], [ag2_in.ap().opt()], [ag2_out.ap().opt()], sCC)
    CC2 = sCC.n

    ag2v3 = ag2_out.ap().rearrange("(r j) t -> r j t", r=NCORE)
    p.wait("sync", sCC, CC2)
    p.wait("sync", sMM, qkv_mm[-1])
    for rblk in range(NCORE):
        p.dma_pid("sync", astage[rblk],
                  lambda pid, rblk=rblk: ag2v3[rblk, :, bass.ts(pid, TOK)], sAS)
    p.wait("sync", sAS)
    astage2 = astage.rearrange("r j t -> (r j) t")
    for uu in range(KC):
        row = 512 * (uu // 4) + 128 * (uu % 4)
        p.dma("sync", xn[:, uu * TOK:(uu + 1) * TOK], astage2[row:row + 128, :], sAT)
    ATTN_ALL = sAT.n

    fo_mm = [0] * KC
    fo_ev = [0] * KC
    fo_st = [0] * KC
    fo_se = [0] * KC
    fo_stats = [0] * KC
    for m in range(KC):
        buf = m % 2
        ob = 6 if buf == 0 else 4
        if m >= 2:
            p.wait("gpsimd", sMM, fo_mm[m - 2])
        elif m == 0:
            p.wait("gpsimd", sMM, qkv_mm[-1])
        p.dma("gpsimd", wst[:, buf * 4096:(buf + 1) * 4096].rearrange("p (k f) -> p k f", k=KC),
              wo_t[m], sW[buf])
        w_lw = sW[buf].n
        if m >= 2:
            p.wait("sync", sEV, fo_ev[m - 2])
        else:
            p.wait("sync", sEV)
        p.wait("sync", sSR[buf], b_store[m])
        p.dma("sync", rin[:, buf * TOK:(buf + 1) * TOK], r1d[m], sR[buf])
        r_ls = sR[buf].n
        p.wait("tensor", sW[buf], w_lw)
        if m == 0:
            p.wait("tensor", sEV, unit_ev[-1])
            p.wait("tensor", sAT, ATTN_ALL)
        if m >= 2:
            p.wait("tensor", sEV, fo_ev[m - 2])
        for k in range(KC):
            for hf in range(2):
                p.mm(bank(ob + hf), wst[:, buf * 4096 + k * 128: buf * 4096 + (k + 1) * 128],
                     xn[:, k * TOK + hf * 512: k * TOK + (hf + 1) * 512],
                     start=(k == 0), stop=(k == KC - 1),
                     sem=(sMM if (k == KC - 1 and hf == 1) else None))
        fo_mm[m] = sMM.n
        p.wait("vector", sMM, fo_mm[m])
        p.wait("vector", sR[buf], r_ls)
        if m >= 2:
            p.wait("vector", sSR[buf], fo_st[m - 2])
            p.wait("vector", sSE, fo_se[m - 2])
        p.v("tensor_add", rosb[:, buf * TOK:(buf + 1) * TOK], bank(ob, 2),
            rin[:, buf * TOK:(buf + 1) * TOK], sem=sEV)
        fo_ev[m] = sEV.n
        p.wait("scalar", sEV, fo_ev[m])
        p.dma("scalar", r2d[m], rosb[:, buf * TOK:(buf + 1) * TOK], sSR[buf])
        fo_st[m] = sSR[buf].n
        if m >= 2:
            p.wait("scalar", sMM, fo_stats[m - 2])
        p.act(rb16[:, buf * TOK:(buf + 1) * TOK], rosb[:, buf * TOK:(buf + 1) * TOK], AF.Copy)
        p.act(sqb[:, buf * TOK:(buf + 1) * TOK], rosb[:, buf * TOK:(buf + 1) * TOK], AF.Square, sem=sSE)
        fo_se[m] = sSE.n

        def emit_fstats(mm_):
            bb_ = mm_ % 2
            p.wait("tensor", sSE, fo_se[mm_])
            for hf in range(2):
                rb = rb16[:, bb_ * TOK + hf * 512: bb_ * TOK + (hf + 1) * 512]
                sq = sqb[:, bb_ * TOK + hf * 512: bb_ * TOK + (hf + 1) * 512]
                p.mm(bank(0 + hf), ones[:, :], rb, start=(mm_ == 0), stop=(mm_ == KC - 1))
                p.mm(bank(2 + hf), ones[:, :], sq, start=(mm_ == 0), stop=(mm_ == KC - 1),
                     sem=(sMM if hf == 1 else None))
            fo_stats[mm_] = sMM.n
        if m >= 1:
            emit_fstats(m - 1)
        if m == KC - 1:
            emit_fstats(m)

    ln_finalize(store_sems=[sSR[0], sSR[1]])
    xn_ev = ln_normalize(r2d, store_gates=fo_st, xn_done_mm=fo_stats[-1],
                         rosb_gate=[(sSR[0], fo_st[KC - 2]), (sSR[1], fo_st[KC - 1])])
    ffn_block(xn_ev, r2d, fo_st, y_out, do_stats=False, w_gate0=fo_mm[-1])

    p.wait("scalar", sSR[0])
    p.wait("scalar", sSR[1])
    p.wait("sync", sSR[0])
    p.wait("sync", sSR[1])

    p.run()
    return nc


_NC_CACHE = [None]


def _prep_inputs(L0_attn_hidden_states, L0_attn_w_qkv, L0_attn_w_o, L0_ffn_W_base,
                 L0_ffn_b_base, L0_ffn_W_gate, L0_ffn_A, L0_ffn_B):
    x = np.asarray(L0_attn_hidden_states, dtype=np.float32)
    w_qkv = np.asarray(L0_attn_w_qkv, dtype=np.float32)
    w_o = np.asarray(L0_attn_w_o, dtype=np.float32)
    W_base = np.asarray(L0_ffn_W_base, dtype=np.float32)
    b_base = np.asarray(L0_ffn_b_base, dtype=np.float32)
    W_gate = np.asarray(L0_ffn_W_gate, dtype=np.float32)
    A = np.asarray(L0_ffn_A, dtype=np.float32)
    Bm = np.asarray(L0_ffn_B, dtype=np.float32)

    wqkv_t = np.ascontiguousarray(w_qkv.reshape(KC, 128, QKVC, 128).transpose(2, 1, 0, 3))
    wo_t = np.ascontiguousarray(w_o.reshape(KC, 128, KC, 128).transpose(2, 1, 0, 3))
    wbase_t = np.ascontiguousarray(W_base.reshape(KC, 128, KC, 128).transpose(0, 3, 2, 1))
    a_t = np.ascontiguousarray(A.reshape(128, KC, 128).transpose(1, 2, 0))
    B_flat = Bm.transpose(0, 2, 1).reshape(128, D)
    b_t = np.ascontiguousarray(B_flat.reshape(128, KC, 128).transpose(1, 0, 2))
    wg_t = np.ascontiguousarray(W_gate.T.reshape(KC, 128, 8))
    bb = np.ascontiguousarray(b_base.reshape(KC, 128).T)
    ones_c = np.ones((128, 128), np.float32)
    ident_c = np.eye(128, dtype=np.float32)
    emat_c = np.zeros((8, 128), np.float32)
    for e in range(8):
        emat_c[e, e * 16:(e + 1) * 16] = 1.0

    slopes = _alibi_slopes(32).reshape(8, 4)
    cc = np.arange(MASKW, dtype=np.int64)[None, :]
    kp = np.arange(128, dtype=np.int64)[:, None]
    rel = cc - 384 - kp
    sqd = np.sqrt(np.maximum(rel, 0).astype(np.float64))
    masks_all = []
    for c in range(NCORE):
        mg = np.empty((G, 128, MASKW), ml_dtypes.bfloat16)
        for g in range(G):
            m = np.exp(-np.float64(slopes[c, g]) * sqd)
            m[rel < 0] = 0.0
            mg[g] = m.astype(ml_dtypes.bfloat16)
        masks_all.append(mg)

    shared = dict(wqkv_t=wqkv_t, wo_t=wo_t, wbase_t=wbase_t, a_t=a_t, b_t=b_t,
                  wg_t=wg_t, bb=bb, ones_c=ones_c, ident_c=ident_c, emat_c=emat_c)
    in_maps = []
    for c in range(NCORE):
        seq, tb = c // 4, c % 4
        xTc = np.ascontiguousarray(
            x[seq, tb * TOK:(tb + 1) * TOK, :].T.reshape(KC, 128, TOK))
        im = dict(shared)
        im["xT"] = xTc
        im["masks"] = masks_all[c]
        in_maps.append(im)
    return in_maps


def kernel(L0_attn_hidden_states, L0_attn_w_qkv, L0_attn_w_o, L0_ffn_W_base,
           L0_ffn_b_base, L0_ffn_W_gate, L0_ffn_A, L0_ffn_B):
    in_maps = _prep_inputs(L0_attn_hidden_states, L0_attn_w_qkv, L0_attn_w_o,
                           L0_ffn_W_base, L0_ffn_b_base, L0_ffn_W_gate, L0_ffn_A, L0_ffn_B)
    if _NC_CACHE[0] is None:
        _NC_CACHE[0] = build_program()
    nc = _NC_CACHE[0]
    res = run_bass_kernel_spmd(nc, in_maps, core_ids=list(range(NCORE)))
    out = np.empty((2, 4096, 4096), np.float32)
    for c in range(NCORE):
        seq, tb = c // 4, c % 4
        y = res.results[c]["y"].reshape(D, TOK)
        out[seq, tb * TOK:(tb + 1) * TOK, :] = y.T
    return out


# version 21
# speedup vs baseline: 1.0735x; 1.0156x over previous
import math
import numpy as np
import ml_dtypes
import concourse.bass as bass
import concourse.mybir as mybir
from concourse.bass_utils import run_bass_kernel_spmd

F32 = mybir.dt.float32
BF16 = mybir.dt.bfloat16
AF = mybir.ActivationFunctionType
ALU = mybir.AluOpType
AX = mybir.AxisListType

D = 4096
TOK = 1024
KC = 32
QKVC = 48
NCORE = 8
G = 4
SEQN = 2
QJN = 8
MASKW = 4480
SCALE = 1.0 / math.sqrt(128.0)
EPS = 1e-5
BIG = 1e30

ENGS = ("sync", "scalar", "vector", "tensor", "gpsimd")


class Sem:
    __slots__ = ("h", "n")

    def __init__(self, nc, name):
        self.h = nc.alloc_semaphore(name)
        self.n = 0


class Prog:
    def __init__(self, nc):
        self.nc = nc
        self.q = {e: [] for e in ENGS}

    def on(self, eng, fn):
        self.q[eng].append(fn)

    def wait(self, eng, sem, val=None):
        v = sem.n if val is None else val
        assert 0 <= v <= sem.n, f"wait on future value {v} > {sem.n}"
        if v <= 0:
            return
        self.q[eng].append(lambda e, h=sem.h, v=v: e.wait_ge(h, v))

    def dma(self, eng, out, in_, sem):
        sem.n += 16
        self.q[eng].append(
            lambda e, o=out, i=in_, h=sem.h: e.dma_start(out=o, in_=i).then_inc(h, 16))

    def dma_pid(self, eng, dst, make_src, sem):
        sem.n += 16

        def f(e, d=dst, ms=make_src, h=sem.h):
            if not hasattr(e, "_pid_cache"):
                e._pid_cache = e.partition_id()
            e.dma_start(out=d, in_=ms(e._pid_cache)).then_inc(h, 16)
        self.q[eng].append(f)

    def mm(self, out, lhsT, rhs, start, stop, sem=None):
        if sem is not None:
            sem.n += 1
        h = sem.h if sem is not None else None

        def f(e, o=out, l=lhsT, r=rhs, s=start, t=stop, h=h):
            ins = e.matmul(o, l, r, start=s, stop=t)
            if h is not None:
                ins.then_inc(h, 1)
        self.q["tensor"].append(f)

    def trans(self, out, in_, ident, sem=None):
        if sem is not None:
            sem.n += 1
        h = sem.h if sem is not None else None

        def f(e, o=out, i=in_, d=ident, h=h):
            ins = e.transpose(o, i, d)
            if h is not None:
                ins.then_inc(h, 1)
        self.q["tensor"].append(f)

    def act(self, out, in_, func, bias=0.0, scale=1.0, sem=None):
        if sem is not None:
            sem.n += 1
        h = sem.h if sem is not None else None

        def f(e, o=out, i=in_, fu=func, b=bias, s=scale, h=h):
            ins = e.activation(out=o, in_=i, func=fu, bias=b, scale=s)
            if h is not None:
                ins.then_inc(h, 1)
        self.q["scalar"].append(f)

    def smul(self, out, in_, c, sem=None):
        if sem is not None:
            sem.n += 1
        h = sem.h if sem is not None else None

        def f(e, o=out, i=in_, c=c, h=h):
            ins = e.mul(o, i, c)
            if h is not None:
                ins.then_inc(h, 1)
        self.q["scalar"].append(f)

    def v(self, opname, *args, sem=None, **kw):
        if sem is not None:
            sem.n += 1
        h = sem.h if sem is not None else None

        def f(e, opname=opname, args=args, kw=kw, h=h):
            ins = getattr(e, opname)(*args, **kw)
            if h is not None:
                ins.then_inc(h, 1)
        self.q["vector"].append(f)

    def collective(self, kind, replica_groups, ins, outs, sem):
        sem.n += 1

        def f(e, k=kind, rg=replica_groups, i=ins, o=outs, h=sem.h):
            e.collective_compute(k, ALU.bypass, replica_groups=rg, ins=i, outs=o).then_inc(h, 1)
        self.q["gpsimd"].append(f)

    def run(self):
        with self.nc.Block() as block:
            for eng in ENGS:
                fns = self.q[eng]
                getattr(block, eng)(lambda e, fns=fns: [fn(e) for fn in fns])


def _alibi_slopes(n):
    cp2 = 2 ** math.floor(math.log2(n))
    base = 2.0 ** (-8.0 / cp2)
    slopes = base ** np.arange(1, cp2 + 1, dtype=np.float32)
    if cp2 != n:
        extra_base = 2.0 ** (-4.0 / cp2)
        extra = extra_base ** np.arange(1, 1 + 2 * (n - cp2), 2, dtype=np.float32)
        slopes = np.concatenate([slopes, extra])
    return slopes.astype(np.float32)


def build_program():
    nc = bass.Bass("TRN2", target_bir_lowering=False)

    xT = nc.dram_tensor("xT", [KC, 128, TOK], F32, kind="ExternalInput").ap()
    wqkv_t = nc.dram_tensor("wqkv_t", [QKVC, 128, KC, 128], F32, kind="ExternalInput").ap()
    wo_t = nc.dram_tensor("wo_t", [KC, 128, KC, 128], F32, kind="ExternalInput").ap()
    wbase_t = nc.dram_tensor("wbase_t", [KC, 128, KC, 128], F32, kind="ExternalInput").ap()
    a_t_in = nc.dram_tensor("a_t", [KC, 128, 128], F32, kind="ExternalInput").ap()
    b_t_in = nc.dram_tensor("b_t", [KC, 128, 128], F32, kind="ExternalInput").ap()
    wg_in = nc.dram_tensor("wg_t", [KC, 128, 8], F32, kind="ExternalInput").ap()
    bb_in = nc.dram_tensor("bb", [128, KC], F32, kind="ExternalInput").ap()
    masks_in = nc.dram_tensor("masks", [G, 128, MASKW], BF16, kind="ExternalInput").ap()
    ones_in = nc.dram_tensor("ones_c", [128, 128], F32, kind="ExternalInput").ap()
    ident_in = nc.dram_tensor("ident_c", [128, 128], F32, kind="ExternalInput").ap()
    emat_in = nc.dram_tensor("emat_c", [8, 128], F32, kind="ExternalInput").ap()

    y_out = nc.dram_tensor("y", [KC, 128, TOK], F32, kind="ExternalOutput").ap()

    r1d = nc.dram_tensor("r1d", [KC, 128, TOK], F32).ap()
    r2d = nc.dram_tensor("r2d", [KC, 128, TOK], F32).ap()
    ag1_in = nc.dram_tensor("ag1_in", [QKVC * 128, TOK], BF16)
    ag1_out = nc.dram_tensor("ag1_out", [NCORE * QKVC * 128, TOK], BF16, addr_space="Shared")
    ag2_in = nc.dram_tensor("ag2_in", [512, NCORE * TOK], BF16)
    ag2_out = nc.dram_tensor("ag2_out", [NCORE * 512, NCORE * TOK], BF16, addr_space="Shared")
    qstage = nc.dram_tensor("qstage", [G, NCORE, 128, TOK], BF16).ap()
    astage = nc.dram_tensor("astage", [NCORE, 512, TOK], BF16).ap()

    sb = nc.alloc_sbuf_tensor
    xn = sb("xn", [128, KC * TOK], BF16)[:]
    wst = sb("wst", [128, 2 * 4096], BF16)[:]
    ksb = sb("ksb", [128, 2 * 4096], BF16)[:]
    vt = sb("vt", [128, 2 * 4096], BF16)[:]
    maskb = sb("maskb", [128, MASKW], BF16)[:]
    a_t = sb("a_tb", [128, KC * 128], BF16)[:]
    b_t = sb("b_tb", [128, KC * 128], BF16)[:]
    wg_t = sb("wg_tb", [128, KC * 8], BF16)[:]
    hsb = sb("hsb", [128, TOK], F32)[:]
    hw = sb("hw", [128, TOK], BF16)[:]
    ltok = sb("ltok", [128, 64], F32)[:]
    m1b = sb("m1b", [128, 8], F32)[:]
    m2b = sb("m2b", [128, 8], F32)[:]
    tm1 = sb("tm1", [128, 64], F32)[:]
    tm2 = sb("tm2", [128, 64], F32)[:]
    l2b = sb("l2b", [128, 64], F32)[:]
    db_ = l2b
    emb = sb("emb", [128, 64], F32)[:]
    edb = tm2
    zb8 = sb("zb8", [128, 8], F32)[:]
    zr8 = sb("zr8", [128, 8], F32)[:]
    rwb = sb("rwb", [128, 64], BF16)[:]
    rwT = sb("rwT", [8, TOK], BF16)[:]
    mstat = sb("mstat", [128, TOK], F32)[:]
    rstd = sb("rstd", [128, TOK], F32)[:]
    rin = sb("rin", [128, 2 * TOK], F32)[:]
    rosb = sb("rosb", [128, 2 * TOK], F32)[:]
    rb16 = sb("rb16", [128, 2 * TOK], BF16)[:]
    sqb = sb("sqb", [128, 2 * TOK], BF16)[:]
    qsl = sb("qsl", [128, 2 * 512], BF16)[:]
    q2b = sb("q2b", [128, 2 * 512], BF16)[:]
    esb = sb("esb", [128, 2 * 512], F32)[:]
    pbf = sb("pbf", [128, 2 * 1536], BF16)[:]
    ebf = sb("ebf", [128, 2 * 1536], BF16)[:]
    zrb = sb("zrb", [128, 2 * 512], F32)[:]
    zlnb = sb("zlnb", [128, 512], F32)[:]
    aob = sb("aob", [128, 2 * 512], BF16)[:]
    negb = mstat[:, 0:64]
    bq1 = sb("bq1", [128, 4], F32)[:]
    k2maxs = sb("k2maxs", [128, 2], F32)[:]
    k2cols = sb("k2cols", [128, 16], F32)[:]
    k2max = sb("k2max", [128, 2], F32)[:]
    ones = sb("ones", [128, 128], BF16)[:]
    identb = sb("identb", [128, 128], BF16)[:]
    identf = sb("identf", [128, 8], F32)[:]
    emat = sb("emat", [8, 128], BF16)[:]
    halfb = sb("halfb", [128, KC], F32)[:]
    scr8k = wst[:, 0:4096]
    gsb = esb[0:8, :]

    ps = nc.alloc_psum_tensor("ps", [128, 8 * 512], F32)[:]

    def bank(i, w=1):
        return ps[:, i * 512:(i + w) * 512]

    p = Prog(nc)
    sMM = Sem(nc, "sMM")
    sT2 = Sem(nc, "sT2")
    sEV = Sem(nc, "sEV")
    sSE = Sem(nc, "sSE")
    sCC = Sem(nc, "sCC")
    sW = [Sem(nc, "sW0"), Sem(nc, "sW1")]
    sR = [Sem(nc, "sR0"), Sem(nc, "sR1")]
    sXB = [Sem(nc, "sXB0"), Sem(nc, "sXB1")]
    sQ = [Sem(nc, "sQ0"), Sem(nc, "sQ1")]
    sSR = [Sem(nc, "sSR0"), Sem(nc, "sSR1")]
    sSQ = [Sem(nc, "sSQ0"), Sem(nc, "sSQ1")]
    sSA = [Sem(nc, "sSA0"), Sem(nc, "sSA1")]
    sC = Sem(nc, "sC")
    sCG = Sem(nc, "sCG")
    sK = Sem(nc, "sK")
    sV = Sem(nc, "sV")
    sQS = Sem(nc, "sQS")
    sMK = Sem(nc, "sMK")
    sAT = Sem(nc, "sAT")
    sAS = Sem(nc, "sAS")

    p.dma("gpsimd", ones[:, :], ones_in, sCG)
    p.dma("gpsimd", identb[:, :], ident_in, sCG)
    p.dma("gpsimd", emat[:, :], emat_in, sCG)
    p.dma("gpsimd", a_t[:].rearrange("p (k f) -> p k f", k=KC), a_t_in.transpose([1, 0, 2]), sCG)
    p.dma("gpsimd", b_t[:].rearrange("p (k f) -> p k f", k=KC), b_t_in.transpose([1, 0, 2]), sCG)
    p.dma("gpsimd", wg_t[:].rearrange("p (k f) -> p k f", k=KC), wg_in.transpose([1, 0, 2]), sCG)
    p.dma("sync", identf[:, :], ident_in[:, 0:8], sC)
    p.dma("sync", esb[:, 0:KC], bb_in, sC)
    p.wait("vector", sC)
    p.v("tensor_scalar_mul", halfb[:, :], esb[:, 0:KC], 0.5, sem=sEV)
    HALFB_EV = sEV.n

    statmm_done = []
    for k in range(KC):
        buf = k % 2
        if k >= 2:
            p.wait("gpsimd", sMM, statmm_done[k - 2])
        p.dma("gpsimd", rb16[:, buf * TOK:(buf + 1) * TOK], xT[k], sXB[buf])
        lw_here = sXB[buf].n
        p.wait("scalar", sXB[buf], lw_here)
        if k >= 2:
            p.wait("scalar", sMM, statmm_done[k - 2])
        p.act(sqb[:, buf * TOK:(buf + 1) * TOK], rb16[:, buf * TOK:(buf + 1) * TOK], AF.Square, sem=sSE)
        se_here = sSE.n
        if k == 0:
            p.wait("tensor", sCG)
        p.wait("tensor", sXB[buf], lw_here)
        p.wait("tensor", sSE, se_here)
        for hf in range(2):
            rb = rb16[:, buf * TOK + hf * 512: buf * TOK + (hf + 1) * 512]
            sq = sqb[:, buf * TOK + hf * 512: buf * TOK + (hf + 1) * 512]
            p.mm(bank(0 + hf), ones[:, :], rb, start=(k == 0), stop=(k == KC - 1))
            p.mm(bank(2 + hf), ones[:, :], sq, start=(k == 0), stop=(k == KC - 1),
                 sem=(sMM if hf == 1 else None))
        statmm_done.append(sMM.n)

    state = {"rstd_se": 0}

    def ln_finalize(store_sems=None):
        T1 = esb[:, 0:512 * 2]
        T2 = rosb[:, 0:TOK]
        p.wait("vector", sMM)
        if store_sems is not None:
            for s_ in store_sems:
                p.wait("vector", s_)
        p.v("tensor_scalar_mul", mstat[:, :], bank(0, 2), 1.0 / D, sem=sEV)
        p.wait("vector", sEV)
        p.v("tensor_mul", T1, mstat[:, :], mstat[:, :], sem=sEV)
        p.wait("vector", sEV)
        p.v("scalar_tensor_tensor", T2, bank(2, 2), 1.0 / D, T1,
            op0=ALU.mult, op1=ALU.subtract, sem=sEV)
        p.wait("vector", sEV)
        p.v("tensor_scalar_add", T1, T2, EPS, sem=sEV)
        p.wait("vector", sEV)
        p.v("reciprocal", T2, T1, sem=sEV)
        p.wait("scalar", sEV)
        p.act(rstd[:, :], T2, AF.Sqrt, sem=sSE)
        state["rstd_se"] = sSE.n

    ln_finalize()

    def ln_normalize(src_dram, store_gates=None, xn_done_mm=None, rosb_gate=None):
        ev_done = []

        def T(k):
            return esb[:, 0:TOK] if k % 2 == 0 else rosb[:, 0:TOK]

        def emit_mult(k):
            p.v("tensor_mul", xn[:, k * TOK:(k + 1) * TOK], T(k), rstd[:, :], sem=sEV)
            ev_done.append(sEV.n)

        for k in range(KC):
            buf = k % 2
            lq = "sync" if buf == 0 else "scalar"
            if k >= 2:
                p.wait(lq, sEV, ev_done[k - 2])
            else:
                p.wait(lq, sEV)
            if store_gates is not None:
                p.wait(lq, sSR[k % 2], store_gates[k])
            p.dma(lq, rin[:, buf * TOK:(buf + 1) * TOK], src_dram[k], sR[buf])
            ls_here = sR[buf].n
            p.wait("vector", sR[buf], ls_here)
            if k == 0:
                p.wait("vector", sSE, state["rstd_se"])
                if xn_done_mm is not None:
                    p.wait("vector", sMM, xn_done_mm)
            if k == 1 and rosb_gate is not None:
                for s_, v_ in rosb_gate:
                    p.wait("vector", s_, v_)
            p.v("tensor_sub", T(k), rin[:, buf * TOK:(buf + 1) * TOK], mstat[:, :])
            if k >= 1:
                emit_mult(k - 1)
        emit_mult(KC - 1)
        return ev_done

    xn_ev = ln_normalize(xT)

    def ffn_block(xn_ev, r_src_dram, r_gates, out_dram, do_stats, w_gate0=0):
        gate_ev = 0
        for hf in range(2):
            for k in range(KC):
                if hf == 0:
                    p.wait("tensor", sEV, xn_ev[k])
                if k == 0 and hf == 1:
                    p.wait("tensor", sEV, gate_ev)
                xk = xn[:, k * TOK + hf * 512: k * TOK + (hf + 1) * 512]
                p.mm(ps[0:8, 4 * 512:4 * 512 + 512], wg_t[:, k * 8:(k + 1) * 8], xk,
                     start=(k == 0), stop=(k == KC - 1))
                p.mm(bank(5), a_t[:, k * 128:(k + 1) * 128], xk,
                     start=(k == 0), stop=(k == KC - 1), sem=(sMM if k == KC - 1 else None))
            mm_here = sMM.n
            p.wait("vector", sMM, mm_here)
            p.v("tensor_copy", gsb[0:8, hf * 512:(hf + 1) * 512], ps[0:8, 4 * 512:4 * 512 + 512], sem=sEV)
            p.wait("vector", sEV)
            p.v("tensor_copy", hsb[:, hf * 512:(hf + 1) * 512], bank(5), sem=sEV)
            p.wait("vector", sEV)
            gate_ev = sEV.n

        p.wait("tensor", sEV, gate_ev)
        for t in range(8):
            p.trans(ps[:, 6 * 512 + t * 8: 6 * 512 + (t + 1) * 8], gsb[0:8, t * 128:(t + 1) * 128],
                    identf[0:8, 0:8], sem=(sMM if t == 7 else None))
        p.wait("vector", sMM)
        p.v("tensor_copy", ltok[:, :], ps[:, 6 * 512:6 * 512 + 64], sem=sEV)
        p.wait("vector", sEV)

        def l3(apx):
            return apx.rearrange("p (b e) -> p b e", b=8)

        def bc(ap8):
            return ap8.unsqueeze(2).broadcast_to([128, 8, 8])

        p.v("reduce_max", m1b[:, :].unsqueeze(2), l3(ltok[:, :]), axis=AX.X, sem=sEV)
        p.wait("vector", sEV)
        p.v("tensor_tensor", l3(tm1[:, :]), l3(ltok[:, :]), bc(m1b[:, :]), op=ALU.is_equal, sem=sEV)
        p.wait("vector", sEV)
        p.v("scalar_tensor_tensor", l2b[:, :], tm1[:, :], -BIG, ltok[:, :],
            op0=ALU.mult, op1=ALU.add, sem=sEV)
        p.wait("vector", sEV)
        p.v("reduce_max", m2b[:, :].unsqueeze(2), l3(l2b[:, :]), axis=AX.X, sem=sEV)
        p.wait("vector", sEV)
        p.v("tensor_tensor", l3(tm2[:, :]), l3(l2b[:, :]), bc(m2b[:, :]), op=ALU.is_equal, sem=sEV)
        p.wait("vector", sEV)
        p.v("tensor_add", tm1[:, :], tm1[:, :], tm2[:, :], sem=sEV)
        p.wait("vector", sEV)
        p.v("tensor_tensor", l3(db_[:, :]), l3(ltok[:, :]), bc(m1b[:, :]), op=ALU.subtract, sem=sEV)
        p.wait("scalar", sEV)
        p.act(edb[:, :], db_[:, :], AF.Exp, sem=sSE)
        p.wait("vector", sSE)
        p.v("tensor_mul", emb[:, :], edb[:, :], tm1[:, :], sem=sEV)
        p.wait("vector", sEV)
        p.v("reduce_sum", zb8[:, :].unsqueeze(2), l3(emb[:, :]), axis=AX.X, sem=sEV)
        p.wait("vector", sEV)
        p.v("reciprocal", zr8[:, :], zb8[:, :], sem=sEV)
        p.wait("vector", sEV)
        p.v("tensor_tensor", l3(rwb[:, :]), l3(emb[:, :]), bc(zr8[:, :]), op=ALU.mult, sem=sEV)
        rw_ev = sEV.n
        p.wait("tensor", sEV, rw_ev)
        rwT_ps = ps[0:8, 6 * 512:7 * 512].bitcast(BF16)
        for t in range(8):
            p.trans(rwT_ps[:, t * 128:(t + 1) * 128], rwb[:, t * 8:(t + 1) * 8], identb[:, :],
                    sem=(sMM if t == 7 else None))
        p.wait("vector", sMM)
        p.v("tensor_copy", rwT[0:8, :], rwT_ps, sem=sEV)
        p.wait("vector", sEV)
        p.wait("tensor", sEV)
        p.mm(bank(6), emat[0:8, :], rwT[0:8, 0:512], start=True, stop=True)
        p.mm(bank(7), emat[0:8, :], rwT[0:8, 512:1024], start=True, stop=True, sem=sMM)
        p.wait("vector", sMM)
        p.wait("vector", sEV, HALFB_EV)
        p.v("tensor_mul", hw[:, 0:512], hsb[:, 0:512], bank(6))
        p.v("tensor_mul", hw[:, 512:1024], hsb[:, 512:1024], bank(7), sem=sEV)
        hw_ev = sEV.n

        evac_ev = [0] * KC
        store_st = [0] * KC
        se_cp = [0] * KC
        stats_mm = [0] * KC
        mm_done = [0] * KC
        for m in range(KC):
            buf = m % 2
            ob = 6 if buf == 0 else 4
            if m >= 2:
                p.wait("gpsimd", sMM, mm_done[m - 2])
            elif m == 0:
                p.wait("gpsimd", sMM, w_gate0)
            p.dma("gpsimd", wst[:, buf * 4096:(buf + 1) * 4096].rearrange("p (k f) -> p k f", k=KC),
                  wbase_t[m], sW[buf])
            w_lw = sW[buf].n
            if m >= 2:
                p.wait("sync", sEV, evac_ev[m - 2])
            else:
                p.wait("sync", sEV)
            if r_gates is not None:
                p.wait("sync", sSR[buf], r_gates[m])
            p.dma("sync", rin[:, buf * TOK:(buf + 1) * TOK], r_src_dram[m], sR[buf])
            r_ls = sR[buf].n
            p.wait("tensor", sW[buf], w_lw)
            if m >= 2:
                p.wait("tensor", sEV, evac_ev[m - 2])
            if m <= 1:
                p.wait("tensor", sEV, hw_ev)
            for k in range(KC):
                for hf in range(2):
                    p.mm(bank(ob + hf), wst[:, buf * 4096 + k * 128: buf * 4096 + (k + 1) * 128],
                         xn[:, k * TOK + hf * 512: k * TOK + (hf + 1) * 512],
                         start=(k == 0), stop=False)
            for hf in range(2):
                p.mm(bank(ob + hf), b_t[:, m * 128:(m + 1) * 128], hw[:, hf * 512:(hf + 1) * 512],
                     start=False, stop=True, sem=(sMM if hf == 1 else None))
            mm_done[m] = sMM.n
            p.wait("vector", sMM, mm_done[m])
            p.wait("vector", sR[buf], r_ls)
            if m >= 2:
                p.wait("vector", sSR[buf], store_st[m - 2])
                if do_stats:
                    p.wait("vector", sSE, se_cp[m - 2])
            p.v("tensor_scalar", esb[:, 0:TOK], bank(ob, 2), 0.5, halfb[:, m:m + 1],
                op0=ALU.mult, op1=ALU.add)
            p.v("tensor_add", rosb[:, buf * TOK:(buf + 1) * TOK], esb[:, 0:TOK],
                rin[:, buf * TOK:(buf + 1) * TOK], sem=sEV)
            evac_ev[m] = sEV.n
            p.wait("scalar", sEV, evac_ev[m])
            p.dma("scalar", out_dram[m], rosb[:, buf * TOK:(buf + 1) * TOK], sSR[buf])
            store_st[m] = sSR[buf].n
            if do_stats:
                if m >= 2:
                    p.wait("scalar", sMM, stats_mm[m - 2])
                p.act(rb16[:, buf * TOK:(buf + 1) * TOK], rosb[:, buf * TOK:(buf + 1) * TOK], AF.Copy)
                p.act(sqb[:, buf * TOK:(buf + 1) * TOK], rosb[:, buf * TOK:(buf + 1) * TOK],
                      AF.Square, sem=sSE)
                se_cp[m] = sSE.n

                def emit_stats(mm_):
                    bb_ = mm_ % 2
                    p.wait("tensor", sSE, se_cp[mm_])
                    for hf in range(2):
                        rb = rb16[:, bb_ * TOK + hf * 512: bb_ * TOK + (hf + 1) * 512]
                        sq = sqb[:, bb_ * TOK + hf * 512: bb_ * TOK + (hf + 1) * 512]
                        p.mm(bank(0 + hf), ones[:, :], rb, start=(mm_ == 0), stop=(mm_ == KC - 1))
                        p.mm(bank(2 + hf), ones[:, :], sq, start=(mm_ == 0), stop=(mm_ == KC - 1),
                             sem=(sMM if hf == 1 else None))
                    stats_mm[mm_] = sMM.n
                if m >= 1:
                    emit_stats(m - 1)
                if m == KC - 1:
                    emit_stats(m)
        return evac_ev, store_st, mm_done, stats_mm

    _, b_store, b_mm, b_stats = ffn_block(xn_ev, xT, None, r1d, do_stats=True)

    ln_finalize(store_sems=[sSR[0], sSR[1]])
    xn_ev = ln_normalize(r1d, store_gates=b_store, xn_done_mm=b_mm[-1],
                         rosb_gate=[(sSR[0], b_store[KC - 2]), (sSR[1], b_store[KC - 1])])

    qkv_mm = [0] * QKVC
    qkv_ev = [0] * QKVC
    qkv_st = [0] * QKVC
    for mi in range(QKVC):
        m = mi
        buf = mi % 2
        ob = 0 if buf == 0 else 2
        if mi >= 2:
            p.wait("gpsimd", sMM, qkv_mm[mi - 2])
        elif mi == 0:
            p.wait("gpsimd", sMM, b_mm[-1])
        p.dma("gpsimd", wst[:, buf * 4096:(buf + 1) * 4096].rearrange("p (k f) -> p k f", k=KC),
              wqkv_t[m], sW[buf])
        w_lw = sW[buf].n
        p.wait("tensor", sW[buf], w_lw)
        if mi >= 2:
            p.wait("tensor", sEV, qkv_ev[mi - 2])
        for k in range(KC):
            if mi == 0:
                p.wait("tensor", sEV, xn_ev[k])
            for hf in range(2):
                p.mm(bank(ob + hf), wst[:, buf * 4096 + k * 128: buf * 4096 + (k + 1) * 128],
                     xn[:, k * TOK + hf * 512: k * TOK + (hf + 1) * 512],
                     start=(k == 0), stop=(k == KC - 1),
                     sem=(sMM if (k == KC - 1 and hf == 1) else None))
        qkv_mm[mi] = sMM.n
        p.wait("vector", sMM, qkv_mm[mi])
        if mi == 0:
            p.wait("vector", sSE)
        if mi >= 2:
            p.wait("vector", sSQ[buf], qkv_st[mi - 2])
        p.v("tensor_copy", rb16[:, buf * TOK:(buf + 1) * TOK], bank(ob, 2), sem=sEV)
        qkv_ev[mi] = sEV.n
        p.wait("scalar", sEV, qkv_ev[mi])
        p.dma("scalar", ag1_in.ap()[m * 128:(m + 1) * 128, :], rb16[:, buf * TOK:(buf + 1) * TOK],
              sSQ[buf])
        qkv_st[mi] = sSQ[buf].n

    p.wait("gpsimd", sSQ[0])
    p.wait("gpsimd", sSQ[1])
    p.collective("AllGather", [list(range(NCORE))], [ag1_in.ap().opt()], [ag1_out.ap().opt()], sCC)
    CC_kv = CC_q = sCC.n

    agv3 = ag1_out.ap().rearrange("(r x) t -> r x t", r=NCORE)
    p.wait("sync", sCC, CC_kv)
    p.dma_pid("sync", ksb[:].rearrange("p (r t) -> p r t", r=8),
              lambda pid: agv3[:, bass.ts(32 + pid, 128), :].transpose([1, 0, 2]), sK)
    p.wait("scalar", sCC, CC_q)
    for g in range(G):
        p.dma_pid("scalar", qstage[g],
                  lambda pid, g=g: agv3[:, bass.ts(4 * pid + g, 128), :], sQS)

    vt_ev = []
    k2_ev = []
    for seq in range(SEQN):
        if seq == 1:
            p.wait("sync", sMM)
        p.dma_pid("sync", scr8k[:].rearrange("p (r t) -> p r t", r=4),
                  lambda pid, seq=seq: agv3[4 * seq:4 * seq + 4,
                                            bass.ts(40 + pid, 128), :].transpose([1, 0, 2]), sV)
        v_ls = sV.n
        vwin = ps[:, 7 * 512:8 * 512].bitcast(BF16)
        p.wait("tensor", sV, v_ls)
        for grp in range(4):
            if grp > 0 or seq > 0:
                p.wait("tensor", sEV)
            for t in range(8):
                kc = grp * 8 + t
                p.trans(vwin[:, t * 128:(t + 1) * 128], scr8k[:, kc * 128:(kc + 1) * 128],
                        identb[:, :], sem=(sMM if t == 7 else None))
            p.wait("vector", sMM)
            p.v("tensor_copy", vt[:, seq * 4096 + grp * 1024: seq * 4096 + (grp + 1) * 1024],
                vwin, sem=sEV)
            p.wait("vector", sEV)
        vt_ev.append(sEV.n)
        p.wait("scalar", sMM)
        p.wait("scalar", sK)
        p.act(scr8k[:, :], ksb[:, seq * 4096:(seq + 1) * 4096], AF.Square, sem=sSE)
        k2sq = sSE.n
        p.wait("tensor", sSE, k2sq)
        for s8 in range(8):
            if s8 > 0 or seq > 0:
                p.wait("tensor", sEV)
            p.mm(bank(6), ones[:, :], scr8k[:, s8 * 512:(s8 + 1) * 512], start=True, stop=True, sem=sMM)
            p.wait("vector", sMM)
            p.v("reduce_max", k2cols[:, seq * 8 + s8: seq * 8 + s8 + 1].unsqueeze(2),
                bank(6).unsqueeze(1), axis=AX.X, sem=sEV)
            p.wait("vector", sEV)
        p.v("reduce_max", k2max[:, seq:seq + 1].unsqueeze(2),
            k2cols[:, seq * 8:(seq + 1) * 8].unsqueeze(1), axis=AX.X, sem=sEV)
        p.wait("vector", sEV)
        k2_ev.append(sEV.n)

    NUNIT = G * SEQN * QJN

    def unit_coords(uu):
        g_, r_ = divmod(uu, SEQN * QJN)
        seq_, qj_ = divmod(r_, QJN)
        return g_, seq_, qj_

    p.wait("vector", sEV, k2_ev[1])
    p.v("tensor_scalar_mul", k2maxs[:, :], k2max[:, :], -SCALE / 2.0, sem=sEV)
    p.wait("vector", sEV)
    p.wait("sync", sQS, 16 * G)
    bq_loads = [0] * NUNIT
    q2_se = [0] * NUNIT
    bmm = [0] * NUNIT
    bq_ev = [0] * NUNIT
    for b in range(NUNIT):
        g_, seq_, qj_ = unit_coords(b)
        buf_ = b % 2
        if b >= 2:
            p.wait("sync", sSE, q2_se[b - 2])
        p.dma("sync", qsl[:, buf_ * 512:(buf_ + 1) * 512],
              qstage[g_, (4 * seq_ + qj_ // 2), :, (qj_ % 2) * 512:(qj_ % 2 + 1) * 512], sQ[buf_])
        bq_loads[b] = sQ[buf_].n
        p.wait("scalar", sQ[buf_], bq_loads[b])
        if b >= 2:
            p.wait("scalar", sMM, bmm[b - 2])
        p.act(q2b[:, buf_ * 512:(buf_ + 1) * 512], qsl[:, buf_ * 512:(buf_ + 1) * 512],
              AF.Square, sem=sSE)
        q2_se[b] = sSE.n
        p.wait("tensor", sSE, q2_se[b])
        if b < 2:
            p.wait("tensor", sEV, qkv_ev[-1])
        else:
            p.wait("tensor", sEV, bq_ev[b - 2])
        p.mm(bank(b % 2), ones[:, :], q2b[:, buf_ * 512:(buf_ + 1) * 512],
             start=True, stop=True, sem=sMM)
        bmm[b] = sMM.n
        p.wait("vector", sMM, bmm[b])
        p.v("reduce_max", bq1[:, (b % 4):(b % 4) + 1].unsqueeze(2), bank(b % 2).unsqueeze(1),
            axis=AX.X, sem=sEV)
        p.wait("vector", sEV)
        p.v("scalar_tensor_tensor", negb[:, b:b + 1], bq1[:, (b % 4):(b % 4) + 1], -SCALE / 2.0,
            k2maxs[:, seq_:seq_ + 1], op0=ALU.mult, op1=ALU.add, sem=sEV)
        bq_ev[b] = sEV.n
        p.wait("vector", sEV)
    bounds_done_ev = sEV.n

    tiles_done_mm = [0] * NUNIT
    unit_ev = [0] * NUNIT
    unit_st = [0] * NUNIT
    ln_se = [0] * NUNIT
    exp_hist = []
    mult_hist = []
    avz_hist = []

    def emit_qload(uu):
        g_, seq_, qj_ = unit_coords(uu)
        buf_ = uu % 2
        if uu >= 2:
            p.wait("sync", sMM, tiles_done_mm[uu - 2])
        else:
            p.wait("sync", sSE, q2_se[NUNIT - 2 + uu])
        p.dma("sync", qsl[:, buf_ * 512:(buf_ + 1) * 512],
              qstage[g_, (4 * seq_ + qj_ // 2), :, (qj_ % 2) * 512:(qj_ % 2 + 1) * 512], sQ[buf_])
        return sQ[buf_].n

    O_BANK = 6
    Z_BANK = 7
    q_ls = [0] * NUNIT
    q_ls[0] = emit_qload(0)
    q_ls[1] = emit_qload(1)

    def emit_avz(meta):
        bi_, eslot_, gb_, u_, seq_, nb_ = meta
        p.wait("tensor", sEV, mult_hist[gb_])
        if bi_ == 0 and u_ >= 1:
            p.wait("tensor", sEV, unit_ev[u_ - 1])
            p.wait("tensor", sSE, ln_se[u_ - 1])
        for t_ in range(2):
            kc_ = 2 * bi_ + t_
            pslice = pbf[:, eslot_ * 1024 + t_ * 512: eslot_ * 1024 + (t_ + 1) * 512]
            p.mm(bank(O_BANK), vt[:, seq_ * 4096 + kc_ * 128: seq_ * 4096 + (kc_ + 1) * 128],
                 pslice, start=(bi_ == 0 and t_ == 0),
                 stop=(bi_ == nb_ - 1 and t_ == 1))
            p.mm(bank(Z_BANK), ones[:, :], pslice,
                 start=(bi_ == 0 and t_ == 0), stop=(bi_ == nb_ - 1 and t_ == 1),
                 sem=(sT2 if t_ == 1 else None))
        avz_hist.append(sT2.n)

    def emit_evac(uu):
        g2, seq2, qj2 = unit_coords(uu)
        buf2 = uu % 2
        p.wait("scalar", sT2, avz_hist[-1])
        if uu >= 2:
            p.wait("scalar", sEV, unit_ev[uu - 2])
        p.act(zlnb[:, :], bank(Z_BANK), AF.Ln)
        p.act(zrb[:, buf2 * 512:(buf2 + 1) * 512], zlnb[:, :], AF.Exp, scale=-1.0, sem=sSE)
        ln_se[uu] = sSE.n
        p.wait("vector", sSE, ln_se[uu])
        if uu >= 2:
            p.wait("vector", sSA[buf2], unit_st[uu - 2])
        p.v("tensor_mul", aob[:, buf2 * 512:(buf2 + 1) * 512], bank(O_BANK),
            zrb[:, buf2 * 512:(buf2 + 1) * 512], sem=sEV)
        unit_ev[uu] = sEV.n
        p.wait("sync", sEV, unit_ev[uu])
        p.dma("sync",
              ag2_in.ap()[g2 * 128:(g2 + 1) * 128,
                          seq2 * 4096 + qj2 * 512: seq2 * 4096 + (qj2 + 1) * 512],
              aob[:, buf2 * 512:(buf2 + 1) * 512], sSA[buf2])
        unit_st[uu] = sSA[buf2].n

    def process_pend(pd):
        emit_avz(pd)
        if pd[0] == pd[5] - 1:
            emit_evac(pd[3])

    gb = 0
    u = 0
    pq = []
    for g in range(G):
        if g > 0:
            p.wait("sync", sEV)
        p.dma("sync", maskb[:, :], masks_in[g], sMK)
        mask_ls = sMK.n
        for seq in range(SEQN):
            for qj in range(QJN):
                buf = u % 2
                ntiles = 4 * (qj + 1)
                nbatch = ntiles // 2
                for bi in range(nbatch):
                    sbase = (gb % 3) * 1024
                    eslot = gb % 3
                    if gb >= 3:
                        p.wait("tensor", sSE, exp_hist[gb - 3])
                    if gb == 0:
                        p.wait("tensor", sEV, bounds_done_ev)
                    p.wait("tensor", sQ[buf], q_ls[u])
                    for t in range(2):
                        kc = 2 * bi + t
                        p.mm(ps[:, sbase + t * 512: sbase + (t + 1) * 512],
                             ksb[:, seq * 4096 + kc * 128: seq * 4096 + (kc + 1) * 128],
                             qsl[:, buf * 512:(buf + 1) * 512], start=True, stop=True,
                             sem=(sMM if t == 1 else None))
                    sc_mm = sMM.n
                    p.wait("scalar", sMM, sc_mm)
                    if gb >= 3:
                        p.wait("scalar", sEV, mult_hist[gb - 3])
                    if gb == 0:
                        p.wait("scalar", sEV, bounds_done_ev)
                    p.act(ebf[:, eslot * 1024: eslot * 1024 + 1024],
                          ps[:, sbase: sbase + 1024], AF.Exp,
                          bias=negb[:, u:u + 1], scale=SCALE, sem=sSE)
                    exp_hist.append(sSE.n)
                    p.wait("vector", sSE, exp_hist[gb])
                    if gb >= 3:
                        p.wait("vector", sT2, avz_hist[gb - 3])
                    if (u == 0 or (g > 0 and seq == 0 and qj == 0)) and bi == 0:
                        p.wait("vector", sMK, mask_ls)
                    for t in range(2):
                        kc = 2 * bi + t
                        off = 512 * qj - 128 * kc + 384
                        p.v("tensor_mul", pbf[:, eslot * 1024 + t * 512: eslot * 1024 + (t + 1) * 512],
                            ebf[:, eslot * 1024 + t * 512: eslot * 1024 + (t + 1) * 512],
                            maskb[:, off:off + 512], sem=(sEV if t == 1 else None))
                    mult_hist.append(sEV.n)
                    if len(pq) == 2:
                        process_pend(pq.pop(0))
                    pq.append((bi, eslot, gb, u, seq, nbatch))
                    gb += 1
                tiles_done_mm[u] = sMM.n
                if u + 2 < NUNIT:
                    q_ls[u + 2] = emit_qload(u + 2)
                u += 1
    for pd in pq:
        process_pend(pd)

    p.wait("gpsimd", sSA[0])
    p.wait("gpsimd", sSA[1])
    p.collective("AllGather", [list(range(NCORE))], [ag2_in.ap().opt()], [ag2_out.ap().opt()], sCC)
    CC2 = sCC.n

    ag2v3 = ag2_out.ap().rearrange("(r j) t -> r j t", r=NCORE)
    p.wait("sync", sCC, CC2)
    p.wait("sync", sMM, qkv_mm[-1])
    for rblk in range(NCORE):
        p.dma_pid("sync", astage[rblk],
                  lambda pid, rblk=rblk: ag2v3[rblk, :, bass.ts(pid, TOK)], sAS)
    p.wait("sync", sAS)
    astage2 = astage.rearrange("r j t -> (r j) t")
    for uu in range(KC):
        row = 512 * (uu // 4) + 128 * (uu % 4)
        p.dma("sync", xn[:, uu * TOK:(uu + 1) * TOK], astage2[row:row + 128, :], sAT)
    ATTN_ALL = sAT.n

    fo_mm = [0] * KC
    fo_ev = [0] * KC
    fo_st = [0] * KC
    fo_se = [0] * KC
    fo_stats = [0] * KC
    for m in range(KC):
        buf = m % 2
        ob = 6 if buf == 0 else 4
        if m >= 2:
            p.wait("gpsimd", sMM, fo_mm[m - 2])
        elif m == 0:
            p.wait("gpsimd", sMM, qkv_mm[-1])
        p.dma("gpsimd", wst[:, buf * 4096:(buf + 1) * 4096].rearrange("p (k f) -> p k f", k=KC),
              wo_t[m], sW[buf])
        w_lw = sW[buf].n
        if m >= 2:
            p.wait("sync", sEV, fo_ev[m - 2])
        else:
            p.wait("sync", sEV)
        p.wait("sync", sSR[buf], b_store[m])
        p.dma("sync", rin[:, buf * TOK:(buf + 1) * TOK], r1d[m], sR[buf])
        r_ls = sR[buf].n
        p.wait("tensor", sW[buf], w_lw)
        if m == 0:
            p.wait("tensor", sEV, unit_ev[-1])
            p.wait("tensor", sAT, ATTN_ALL)
        if m >= 2:
            p.wait("tensor", sEV, fo_ev[m - 2])
        for k in range(KC):
            for hf in range(2):
                p.mm(bank(ob + hf), wst[:, buf * 4096 + k * 128: buf * 4096 + (k + 1) * 128],
                     xn[:, k * TOK + hf * 512: k * TOK + (hf + 1) * 512],
                     start=(k == 0), stop=(k == KC - 1),
                     sem=(sMM if (k == KC - 1 and hf == 1) else None))
        fo_mm[m] = sMM.n
        p.wait("vector", sMM, fo_mm[m])
        p.wait("vector", sR[buf], r_ls)
        if m >= 2:
            p.wait("vector", sSR[buf], fo_st[m - 2])
            p.wait("vector", sSE, fo_se[m - 2])
        p.v("tensor_add", rosb[:, buf * TOK:(buf + 1) * TOK], bank(ob, 2),
            rin[:, buf * TOK:(buf + 1) * TOK], sem=sEV)
        fo_ev[m] = sEV.n
        p.wait("scalar", sEV, fo_ev[m])
        p.dma("scalar", r2d[m], rosb[:, buf * TOK:(buf + 1) * TOK], sSR[buf])
        fo_st[m] = sSR[buf].n
        if m >= 2:
            p.wait("scalar", sMM, fo_stats[m - 2])
        p.act(rb16[:, buf * TOK:(buf + 1) * TOK], rosb[:, buf * TOK:(buf + 1) * TOK], AF.Copy)
        p.act(sqb[:, buf * TOK:(buf + 1) * TOK], rosb[:, buf * TOK:(buf + 1) * TOK], AF.Square, sem=sSE)
        fo_se[m] = sSE.n

        def emit_fstats(mm_):
            bb_ = mm_ % 2
            p.wait("tensor", sSE, fo_se[mm_])
            for hf in range(2):
                rb = rb16[:, bb_ * TOK + hf * 512: bb_ * TOK + (hf + 1) * 512]
                sq = sqb[:, bb_ * TOK + hf * 512: bb_ * TOK + (hf + 1) * 512]
                p.mm(bank(0 + hf), ones[:, :], rb, start=(mm_ == 0), stop=(mm_ == KC - 1))
                p.mm(bank(2 + hf), ones[:, :], sq, start=(mm_ == 0), stop=(mm_ == KC - 1),
                     sem=(sMM if hf == 1 else None))
            fo_stats[mm_] = sMM.n
        if m >= 1:
            emit_fstats(m - 1)
        if m == KC - 1:
            emit_fstats(m)

    ln_finalize(store_sems=[sSR[0], sSR[1]])
    xn_ev = ln_normalize(r2d, store_gates=fo_st, xn_done_mm=fo_stats[-1],
                         rosb_gate=[(sSR[0], fo_st[KC - 2]), (sSR[1], fo_st[KC - 1])])
    ffn_block(xn_ev, r2d, fo_st, y_out, do_stats=False, w_gate0=fo_mm[-1])

    p.wait("scalar", sSR[0])
    p.wait("scalar", sSR[1])
    p.wait("sync", sSR[0])
    p.wait("sync", sSR[1])

    p.run()
    return nc


_NC_CACHE = [None]


def _prep_inputs(L0_attn_hidden_states, L0_attn_w_qkv, L0_attn_w_o, L0_ffn_W_base,
                 L0_ffn_b_base, L0_ffn_W_gate, L0_ffn_A, L0_ffn_B):
    x = np.asarray(L0_attn_hidden_states, dtype=np.float32)
    w_qkv = np.asarray(L0_attn_w_qkv, dtype=np.float32)
    w_o = np.asarray(L0_attn_w_o, dtype=np.float32)
    W_base = np.asarray(L0_ffn_W_base, dtype=np.float32)
    b_base = np.asarray(L0_ffn_b_base, dtype=np.float32)
    W_gate = np.asarray(L0_ffn_W_gate, dtype=np.float32)
    A = np.asarray(L0_ffn_A, dtype=np.float32)
    Bm = np.asarray(L0_ffn_B, dtype=np.float32)

    wqkv_t = np.ascontiguousarray(w_qkv.reshape(KC, 128, QKVC, 128).transpose(2, 1, 0, 3))
    wo_t = np.ascontiguousarray(w_o.reshape(KC, 128, KC, 128).transpose(2, 1, 0, 3))
    wbase_t = np.ascontiguousarray(W_base.reshape(KC, 128, KC, 128).transpose(0, 3, 2, 1))
    a_t = np.ascontiguousarray(A.reshape(128, KC, 128).transpose(1, 2, 0))
    B_flat = Bm.transpose(0, 2, 1).reshape(128, D)
    b_t = np.ascontiguousarray(B_flat.reshape(128, KC, 128).transpose(1, 0, 2))
    wg_t = np.ascontiguousarray(W_gate.T.reshape(KC, 128, 8))
    bb = np.ascontiguousarray(b_base.reshape(KC, 128).T)
    ones_c = np.ones((128, 128), np.float32)
    ident_c = np.eye(128, dtype=np.float32)
    emat_c = np.zeros((8, 128), np.float32)
    for e in range(8):
        emat_c[e, e * 16:(e + 1) * 16] = 1.0

    slopes = _alibi_slopes(32).reshape(8, 4)
    cc = np.arange(MASKW, dtype=np.int64)[None, :]
    kp = np.arange(128, dtype=np.int64)[:, None]
    rel = cc - 384 - kp
    sqd = np.sqrt(np.maximum(rel, 0).astype(np.float64))
    masks_all = []
    for c in range(NCORE):
        mg = np.empty((G, 128, MASKW), ml_dtypes.bfloat16)
        for g in range(G):
            m = np.exp(-np.float64(slopes[c, g]) * sqd)
            m[rel < 0] = 0.0
            mg[g] = m.astype(ml_dtypes.bfloat16)
        masks_all.append(mg)

    shared = dict(wqkv_t=wqkv_t, wo_t=wo_t, wbase_t=wbase_t, a_t=a_t, b_t=b_t,
                  wg_t=wg_t, bb=bb, ones_c=ones_c, ident_c=ident_c, emat_c=emat_c)
    in_maps = []
    for c in range(NCORE):
        seq, tb = c // 4, c % 4
        xTc = np.ascontiguousarray(
            x[seq, tb * TOK:(tb + 1) * TOK, :].T.reshape(KC, 128, TOK))
        im = dict(shared)
        im["xT"] = xTc
        im["masks"] = masks_all[c]
        in_maps.append(im)
    return in_maps


def kernel(L0_attn_hidden_states, L0_attn_w_qkv, L0_attn_w_o, L0_ffn_W_base,
           L0_ffn_b_base, L0_ffn_W_gate, L0_ffn_A, L0_ffn_B):
    in_maps = _prep_inputs(L0_attn_hidden_states, L0_attn_w_qkv, L0_attn_w_o,
                           L0_ffn_W_base, L0_ffn_b_base, L0_ffn_W_gate, L0_ffn_A, L0_ffn_B)
    if _NC_CACHE[0] is None:
        _NC_CACHE[0] = build_program()
    nc = _NC_CACHE[0]
    res = run_bass_kernel_spmd(nc, in_maps, core_ids=list(range(NCORE)))
    out = np.empty((2, 4096, 4096), np.float32)
    for c in range(NCORE):
        seq, tb = c // 4, c % 4
        y = res.results[c]["y"].reshape(D, TOK)
        out[seq, tb * TOK:(tb + 1) * TOK, :] = y.T
    return out
